# version 02
# baseline (speedup 1.0000x reference)
import numpy as np
import ml_dtypes
from contextlib import ExitStack
import concourse.bass as bass
from concourse.bass_utils import run_bass_kernel_spmd
import concourse.mybir as mybir

F32 = mybir.dt.float32
BF16 = mybir.dt.bfloat16
I32 = mybir.dt.int32
AF = mybir.ActivationFunctionType
ALU = mybir.AluOpType
AX = mybir.AxisListType

D = 1024
S = 2048
NB = 2
ROWS = NB * S
NT = ROWS // 128
NH = 16
FF = 3072
VW = NH * 65


DBG = {}

def emit_l0(nc, block, es, x, x1, W, pref="l0", final_wait=True):
    def sbt(name, shape, dt, scope=None):
        t = (scope or es).enter_context(nc.sbuf_tensor(pref + name, shape, dt))
        DBG[pref + name] = t
        return t

    sems = {}
    for n in ["ldc", "lw", "wcast", "ax", "anm", "astd", "axn", "atr", "aev",
              "xnd", "bxs", "bmm", "bev", "bvm", "csq", "cnmm", "cprep", "cst",
              "cexp", "cpool", "cav", "crow", "crbc", "cobc", "cosh", "cwo",
              "cwoev", "cattn", "dwl", "dwc", "dxs", "dmm", "dsilu", "dhv",
              "dh", "ewl", "ewc", "ehl", "etr", "etev", "emm", "efin", "est",
              "ax1", "lw1", "dxs1", "dh1", "ehl1", "est1", "bxs1", "cattn1", "rs"]:
        sems[n] = es.enter_context(nc.semaphore(pref + n))

    s_ident = sbt("ident", [128, 128], F32)
    s_identb = sbt("identb", [128, 128], BF16)
    s_ones = sbt("ones", [128, 128], BF16)
    s_onesf = sbt("onesf", [128, 128], F32)
    s_tri = sbt("tri", [128, 128], BF16)
    s_eps = sbt("eps", [128, 1], F32)
    s_eps5 = sbt("eps5", [128, 1], F32)
    s_bq = sbt("bq", [128, 8], F32)
    s_bk = sbt("bk", [128, 8], F32)
    s_bvr = sbt("bvr", [1, D], BF16)
    s_bor = sbt("bor", [1, D], BF16)
    s_negCh = sbt("negCh", [128, 16], F32)
    s_c1 = sbt("c1", [128, 40], F32)

    ps_a = es.enter_context(nc.psum_tensor(pref + "ps_a", [128, 512], F32))
    ps_b = es.enter_context(nc.psum_tensor(pref + "ps_b", [128, 512], F32))
    ps_o = es.enter_context(nc.psum_tensor(pref + "ps_o", [128, S], F32))
    ps_t0 = es.enter_context(nc.psum_tensor(pref + "ps_t0", [128, 128], BF16))
    ps_t1 = es.enter_context(nc.psum_tensor(pref + "ps_t1", [128, 128], BF16))
    ps_ab = [ps_a, ps_b]
    ps_tt = [ps_t0, ps_t1]

    xnT_h = nc.dram_tensor(pref + "xnT_h", [D, ROWS], BF16).ap()
    attn_h = nc.dram_tensor(pref + "attn_h", [ROWS, D], F32).ap()
    h_h = nc.dram_tensor(pref + "h_h", [ROWS, FF], BF16).ap()

    scA = ExitStack()
    sc_x = sbt("A_x", [128, 2 * D], F32, scA)
    sc_t = sbt("A_t", [128, D], F32, scA)
    sc_t2 = sbt("A_t2", [128, D], F32, scA)
    sc_st = sbt("A_st", [128, 12], F32, scA)
    sc_xn = sbt("A_xn", [128, D], BF16, scA)
    sc_tst = sbt("A_tst", [128, 8 * 512], BF16, scA)

    @block.sync
    def _(sync):
        for nm, dst in [("ident_f32", s_ident), ("ident_bf", s_identb),
                        ("ones_bf", s_ones), ("ones_f32", s_onesf),
                        ("trimask", s_tri), ("epshuge", s_eps), ("eps5", s_eps5),
                        ("bqc", s_bq), ("bkc", s_bk)]:
            sync.dma_start(out=dst[:], in_=W[nm][:, :]).then_inc(sems["ldc"], 16)
        sync.dma_start(out=sc_t[0:1, 0:D], in_=W["bv_row"][:, :]).then_inc(sems["ldc"], 16)
        sync.dma_start(out=sc_t2[0:1, 0:D], in_=W["bo_row"][:, :]).then_inc(sems["ldc"], 16)
        def stores(g):
            sync.wait_ge(sems["aev"], (g * 4 + 4) * 8)
            for ic in range(8):
                sync.dma_start(out=xnT_h[ic * 128:(ic + 1) * 128, g * 512:(g + 1) * 512],
                               in_=sc_tst[:, ic * 512:(ic + 1) * 512]).then_inc(sems["xnd"], 16)

        for r in range(NT):
            if r >= 2:
                sync.wait_ge(sems["axn"], r - 1)
            sync.dma_start(out=sc_x[:, (r % 2) * D:(r % 2 + 1) * D],
                           in_=x[r * 128:(r + 1) * 128, :]).then_inc(
                               sems["ax" if r % 2 == 0 else "ax1"], 16)
            if r >= 4 and r % 4 == 0:
                stores(r // 4 - 1)
        stores(NT // 4 - 1)
    N_LDC = 11 * 16
    N_XND = (NT // 4) * 8 * 16

    @block.vector
    def _(vector):
        vector.wait_ge(sems["ldc"], N_LDC)
        vector.tensor_scalar(s_bq[:], s_bq[:], 0.125, None, ALU.mult)
        vector.tensor_copy(s_bvr[:], sc_t[0:1, 0:D])
        vector.tensor_copy(s_bor[:], sc_t2[0:1, 0:D]).then_inc(sems["wcast"], 1)
        rsn = [0]
        for r in range(NT):
            nmc = 1 if r % 2 == 0 else 8
            xt = sc_x[:, (r % 2) * D:(r % 2 + 1) * D]
            vector.wait_ge(sems["ax" if r % 2 == 0 else "ax1"], (r // 2 + 1) * 16)
            vector.tensor_reduce(sc_st[:, 0:1], xt, axis=AX.X,
                                 op=ALU.add).then_inc(sems["rs"], 1)
            rsn[0] += 1
            vector.wait_ge(sems["rs"], rsn[0])
            vector.tensor_scalar(sc_st[:, nmc:nmc + 1], sc_st[:, 0:1], -1.0 / D,
                                 None, ALU.mult).then_inc(sems["rs"], 1)
            rsn[0] += 1
            vector.wait_ge(sems["rs"], rsn[0])
            vector.tensor_scalar(sc_t[:], xt, sc_st[:, nmc:nmc + 1], None, ALU.add)
            vector.tensor_tensor(sc_t2[:], sc_t[:], sc_t[:], ALU.mult)
            vector.tensor_reduce(sc_st[:, 2:3], sc_t2[:], axis=AX.X,
                                 op=ALU.add).then_inc(sems["anm"], 1)
            vector.wait_ge(sems["astd"], r + 1)
            if r >= 1:
                vector.wait_ge(sems["atr"], r * 8)
            vector.tensor_scalar(sc_xn[:], sc_t[:], sc_st[:, 4:5], None,
                                 ALU.mult).then_inc(sems["axn"], 1)

    @block.scalar
    def _(scalar):
        for r in range(NT):
            scalar.wait_ge(sems["anm"], r + 1)
            scalar.activation(sc_st[:, 3:4], sc_st[:, 2:3], AF.Ln,
                              scale=1.0 / D, bias=s_eps5[:])
            scalar.activation(sc_st[:, 4:5], sc_st[:, 3:4], AF.Exp,
                              scale=-0.5).then_inc(sems["astd"], 1)
            for ic in range(8):
                k = r * 8 + ic
                scalar.wait_ge(sems["atr"], k + 1)
                scalar.copy(sc_tst[:, ic * 512 + (r % 4) * 128:ic * 512 + (r % 4) * 128 + 128],
                            ps_tt[ic % 2][:]).then_inc(sems["aev"], 1)

    @block.tensor
    def _(tensor):
        for r in range(NT):
            tensor.wait_ge(sems["axn"], r + 1)
            if r % 4 == 0 and r > 0:
                tensor.wait_ge(sems["xnd"], (r // 4) * 8 * 16)
            for ic in range(8):
                k = r * 8 + ic
                if k >= 2:
                    tensor.wait_ge(sems["aev"], k - 1)
                tensor.transpose(ps_tt[ic % 2][:],
                                 sc_xn[:, ic * 128:(ic + 1) * 128],
                                 s_identb[:]).then_inc(sems["atr"], 1)
    scA.close()

    c = {k: 0 for k in sems}
    c.update(ldc=N_LDC, xnd=N_XND, wcast=1, axn=NT, aev=NT * 8, atr=NT * 8,
             anm=NT, astd=NT, ax=NT * 16)

    for b in range(NB):
        scBC = ExitStack()
        s_q2h = sbt("q2h%d" % b, [128, 8 * S], BF16, scBC)
        s_k2h = sbt("k2h%d" % b, [128, 8 * S], BF16, scBC)
        s_vtok = sbt("vtok%d" % b, [128, 16 * VW], BF16, scBC)

        scB = ExitStack()
        s_wq = sbt("wq%d" % b, [128, 8 * D], BF16, scB)
        s_wk = sbt("wk%d" % b, [128, 8 * D], BF16, scB)
        s_wv = sbt("wv%d" % b, [128, 8 * D], BF16, scB)
        s_wst = sbt("wst%d" % b, [128, 2 * D], F32, scB)
        s_xsl = sbt("xsl%d" % b, [128, 16 * 512], BF16, scB)

        wsrc = [W["wqT"], W["wkT"], W["wvT"]]
        wdst = [s_wq, s_wk, s_wv]
        lw0 = (c["lw"], c["lw1"])
        bxs0 = (c["bxs"], c["bxs1"])
        wc0, bev0, bvm0, bmm0 = (c[k] for k in
                                 ("wcast", "bev", "bvm", "bmm"))
        cattn_prev = (c["cattn"], c["cattn1"])

        @block.gpsimd
        def _(g, b=b, wsrc=wsrc, lw0=lw0, wc0=wc0, s_wst=s_wst, cattn_prev=cattn_prev):
            if b == 0:
                g.wait_ge(sems["xnd"], N_XND)
            else:
                g.wait_ge(sems["cattn"], cattn_prev[0])
                g.wait_ge(sems["cattn1"], cattn_prev[1])
            for wi in range(3):
                for ic in range(8):
                    j = wi * 8 + ic
                    if j >= 2:
                        g.wait_ge(sems["wcast"], wc0 + j - 1)
                    g.dma_start(out=s_wst[:, (j % 2) * D:(j % 2 + 1) * D],
                                in_=wsrc[wi][ic * 128:(ic + 1) * 128, :]).then_inc(
                                    sems["lw" if j % 2 == 0 else "lw1"], 16)
        c["lw"] += 12 * 16
        c["lw1"] += 12 * 16

        @block.vector
        def _(v, b=b, wdst=wdst, lw0=lw0, s_wst=s_wst, s_vtok=s_vtok):
            for wi in range(3):
                for ic in range(8):
                    j = wi * 8 + ic
                    v.wait_ge(sems["lw" if j % 2 == 0 else "lw1"],
                              lw0[j % 2] + (j // 2 + 1) * 16)
                    v.tensor_copy(wdst[wi][:, ic * D:(ic + 1) * D],
                                  s_wst[:, (j % 2) * D:(j % 2 + 1) * D]).then_inc(sems["wcast"], 1)
            for kti in range(16):
                vv = s_vtok[:, kti * VW:(kti + 1) * VW].rearrange("p (h e) -> p h e", e=65)
                v.memset(vv[:, :, 64:65], 1.0).then_inc(sems["bvm"], 1)
        c["wcast"] += 24
        c["bvm"] += 16

        @block.sync
        def _(sync, b=b, bxs0=bxs0, bmm0=bmm0, s_xsl=s_xsl, cattn_prev=cattn_prev):
            if b == 0:
                sync.wait_ge(sems["xnd"], N_XND)
            else:
                sync.wait_ge(sems["cattn"], cattn_prev[0])
                sync.wait_ge(sems["cattn1"], cattn_prev[1])
            for tc in range(4):
                if tc >= 2:
                    sync.wait_ge(sems["bmm"], bmm0 + (tc - 1) * 24)
                for ic in range(8):
                    sync.dma_start(
                        out=s_xsl[:, ((tc % 2) * 8 + ic) * 512:((tc % 2) * 8 + ic + 1) * 512],
                        in_=xnT_h[ic * 128:(ic + 1) * 128,
                                  b * S + tc * 512:b * S + (tc + 1) * 512]).then_inc(
                            sems["bxs" if tc % 2 == 0 else "bxs1"], 16)
        c["bxs"] += 16 * 16
        c["bxs1"] += 16 * 16

        @block.tensor
        def _(t, b=b, bxs0=bxs0, bev0=bev0, bvm0=bvm0, wcA=c["wcast"],
              s_wq=s_wq, s_wk=s_wk, s_wv=s_wv, s_xsl=s_xsl):
            t.wait_ge(sems["wcast"], wcA)
            t.wait_ge(sems["bvm"], bvm0 + 16)
            pu = 0
            for tc in range(4):
                t.wait_ge(sems["bxs" if tc % 2 == 0 else "bxs1"],
                          bxs0[tc % 2] + (tc // 2 + 1) * 8 * 16)
                xb = (tc % 2) * 8
                for wsb in (s_wq, s_wk):
                    for oc in range(8):
                        if pu >= 2:
                            t.wait_ge(sems["bev"], bev0 + pu - 1)
                        pp = ps_ab[pu % 2]
                        for ic in range(8):
                            mi = t.matmul(pp[:, 0:512],
                                          wsb[:, ic * D + oc * 128:ic * D + oc * 128 + 128],
                                          s_xsl[:, (xb + ic) * 512:(xb + ic + 1) * 512],
                                          start=(ic == 0), stop=(ic == 7))
                        mi.then_inc(sems["bmm"], 1)
                        pu += 1
                for kt in range(4):
                    for oc5 in range(2):
                        if pu >= 2:
                            t.wait_ge(sems["bev"], bev0 + pu - 1)
                        pp = ps_ab[pu % 2]
                        for ic in range(8):
                            t.matmul(pp[:, 0:512],
                                     s_xsl[:, (xb + ic) * 512 + kt * 128:(xb + ic) * 512 + kt * 128 + 128],
                                     s_wv[:, ic * D + oc5 * 512:ic * D + (oc5 + 1) * 512],
                                     start=(ic == 0), stop=False)
                        mi = t.matmul(pp[:, 0:512], s_ones[0:1, 0:128],
                                      s_bvr[0:1, oc5 * 512:(oc5 + 1) * 512],
                                      start=False, stop=True)
                        mi.then_inc(sems["bmm"], 1)
                        pu += 1
        c["bmm"] += 96

        @block.scalar
        def _(s, b=b, bmm0=bmm0, s_q2h=s_q2h, s_k2h=s_k2h, s_vtok=s_vtok):
            pu = 0
            for tc in range(4):
                for dst2h, scale, bias in [(s_q2h, 0.125, s_bq), (s_k2h, 1.0, s_bk)]:
                    for oc in range(8):
                        s.wait_ge(sems["bmm"], bmm0 + pu + 1)
                        s.activation(dst2h[:, oc * S + tc * 512:oc * S + (tc + 1) * 512],
                                     ps_ab[pu % 2][:, 0:512], AF.Identity, scale=scale,
                                     bias=bias[:, oc:oc + 1]).then_inc(sems["bev"], 1)
                        pu += 1
                for kt in range(4):
                    kti = tc * 4 + kt
                    for oc5 in range(2):
                        s.wait_ge(sems["bmm"], bmm0 + pu + 1)
                        for hh in range(8):
                            h = oc5 * 8 + hh
                            mi = s.copy(s_vtok[:, kti * VW + 65 * h:kti * VW + 65 * h + 64],
                                        ps_ab[pu % 2][:, hh * 64:(hh + 1) * 64])
                        mi.then_inc(sems["bev"], 1)
                        pu += 1
        c["bev"] += 96
        scB.close()

        scC = ExitStack()
        s_wo = sbt("wo%d" % b, [128, 8 * D], BF16, scC)
        s_wst2 = sbt("wst2%d" % b, [128, D], F32, scC)
        s_scr = sbt("scr%d" % b, [128, S], F32, scC)
        s_sqb = sbt("sqb%d" % b, [128, S], BF16, scC)
        s_rowb = sbt("rowb%d" % b, [128, S], BF16, scC)
        s_pt = sbt("pt%d" % b, [128, 2 * S], BF16, scC)
        s_rbc = sbt("rbc%d" % b, [128, 2 * 512], BF16, scC)
        s_osh = sbt("osh%d" % b, [128, S], BF16, scC)
        s_ofm = sbt("ofm%d" % b, [128, 8 * S], BF16, scC)
        s_att = sbt("att%d" % b, [128, 2 * 512], F32, scC)

        lw0, wc0 = c["lw"], c["wcast"]

        @block.gpsimd
        def _(g, lw0=lw0, wc0=wc0, s_wst2=s_wst2, bev_all=c["bev"]):
            g.wait_ge(sems["bev"], bev_all)
            for ic in range(8):
                if ic >= 1:
                    g.wait_ge(sems["wcast"], wc0 + ic)
                g.dma_start(out=s_wst2[:],
                            in_=W["woT"][ic * 128:(ic + 1) * 128, :]).then_inc(sems["lw"], 16)
        c["lw"] += 8 * 16

        @block.vector
        def _(v, lw0=lw0, s_wst2=s_wst2, s_wo=s_wo):
            for ic in range(8):
                v.wait_ge(sems["lw"], lw0 + (ic + 1) * 16)
                v.tensor_copy(s_wo[:, ic * D:(ic + 1) * D], s_wst2[:]).then_inc(sems["wcast"], 1)
        c["wcast"] += 8

        csq0, cnmm0, cprep0, crow0, crbc0 = (c[k] for k in
                                             ("csq", "cnmm", "cprep", "crow", "crbc"))

        @block.scalar
        def _(s, csq0=csq0, cnmm0=cnmm0, cprep0=cprep0, crow0=crow0, bev_all=c["bev"],
              s_q2h=s_q2h, s_k2h=s_k2h, s_sqb=s_sqb):
            s.wait_ge(sems["bev"], bev_all)
            for si, src in enumerate((s_q2h, s_k2h)):
                for ic in range(8):
                    j = si * 8 + ic
                    if j >= 1:
                        s.wait_ge(sems["cnmm"], cnmm0 + j * 2)
                    s.activation(s_sqb[:], src[:, ic * S:(ic + 1) * S],
                                 AF.Square).then_inc(sems["csq"], 1)
            for h in range(NH):
                s.wait_ge(sems["cprep"], cprep0 + 32 + 3 * h + 1)
                s.activation(s_c1[0:1, 33:34], s_c1[0:1, 32:33],
                             AF.Sqrt).then_inc(sems["crow"], 1)

        c["csq"] += 16

        @block.tensor
        def _(t, csq0=csq0, cprep0=cprep0, crbc0=crbc0, s_sqb=s_sqb):
            for si in range(2):
                for ic in range(8):
                    j = si * 8 + ic
                    t.wait_ge(sems["csq"], csq0 + j + 1)
                    for hh in range(2):
                        hidx = j * 2 + hh
                        if hidx >= 1:
                            t.wait_ge(sems["cprep"], cprep0 + hidx)
                        r0 = hh * 64
                        for jj in range(4):
                            mi = t.matmul(ps_o[0:1, jj * 512:(jj + 1) * 512],
                                          s_ones[r0:r0 + 64, 0:1],
                                          s_sqb[r0:r0 + 64, jj * 512:(jj + 1) * 512],
                                          start=True, stop=True,
                                          skip_group_check=True)
                        mi.then_inc(sems["cnmm"], 1)
            for h in range(NH):
                t.wait_ge(sems["cprep"], cprep0 + 32 + 3 * h + 2)
                t.matmul(ps_a[:, 0:1], s_onesf[0:1, 0:128], s_c1[0:1, 34:35],
                         start=True, stop=True).then_inc(sems["crbc"], 1)
        c["cnmm"] += 32
        c["crbc"] += NH

        @block.vector
        def _(v, cnmm0=cnmm0, cprep0=cprep0, crow0=crow0, crbc0=crbc0, s_negCh=s_negCh):
            for hidx in range(32):
                v.wait_ge(sems["cnmm"], cnmm0 + hidx + 1)
                v.tensor_reduce(s_c1[0:1, hidx:hidx + 1], ps_o[0:1, 0:S], axis=AX.X,
                                op=ALU.max).then_inc(sems["cprep"], 1)
            v.wait_ge(sems["cprep"], cprep0 + 32)
            for h in range(NH):
                v.tensor_tensor(s_c1[0:1, 32:33], s_c1[0:1, h:h + 1],
                                s_c1[0:1, 16 + h:17 + h], ALU.mult).then_inc(sems["cprep"], 1)
                v.wait_ge(sems["crow"], crow0 + h + 1)
                v.tensor_scalar(s_c1[0:1, 34:35], s_c1[0:1, 33:34], -1.01, None,
                                ALU.mult).then_inc(sems["cprep"], 1)
                v.wait_ge(sems["crbc"], crbc0 + h + 1)
                v.tensor_copy(s_negCh[:, h:h + 1], ps_a[:, 0:1]).then_inc(sems["cprep"], 1)
        c["cprep"] += 32 + 3 * NH
        c["crow"] += NH


        cst0, cexp0, cpool0, cav0 = (c[k] for k in ("cst", "cexp", "cpool", "cav"))
        crow1, crbc1 = c["crow"], c["crbc"]
        cobc0, cosh0 = c["cobc"], c["cosh"]

        def chunks_of(ci):
            qlen = S - 128 * ci
            out = []
            off = 0
            while off < qlen:
                L = min(512, qlen - off)
                out.append((off, L))
                off += L
            return out

        headplan = []
        st_u = 0
        for h in range(NH):
            cis = []
            for ci in range(16):
                chs = chunks_of(ci)
                cis.append(dict(ci=ci, chs=chs, st0=st_u, nch=len(chs)))
                st_u += len(chs)
            headplan.append(dict(h=h, cis=cis))
        ST_TOTAL = st_u

        @block.tensor
        def _(t, b=b, headplan=headplan, cst0=cst0, cexp0=cexp0, cpool0=cpool0,
              cav0=cav0, crow1=crow1, cobc0=cobc0, cprepA=c["cprep"],
              s_q2h=s_q2h, s_k2h=s_k2h, s_vtok=s_vtok, s_rowb=s_rowb):
            t.wait_ge(sems["cprep"], cprepA)
            for hi, plan in enumerate(headplan):
                h = plan["h"]; h2 = h // 2; r0 = (h % 2) * 64
                t.wait_ge(sems["cobc"], cobc0 + hi * 4)

                def emit_ST(cid):
                    ci = cid["ci"]
                    for k, (off, L) in enumerate(cid["chs"]):
                        su = cid["st0"] + k
                        if su >= 2:
                            t.wait_ge(sems["cexp"], cexp0 + hi * 0 + su - 1)
                        t.matmul(ps_ab[su % 2][:, 0:L],
                                 s_k2h[r0:r0 + 64, h2 * S + ci * 128:h2 * S + (ci + 1) * 128],
                                 s_q2h[r0:r0 + 64, h2 * S + ci * 128 + off:h2 * S + ci * 128 + off + L],
                                 start=True, stop=True).then_inc(sems["cst"], 1)

                def emit_AV(cid):
                    ci = cid["ci"]
                    t.wait_ge(sems["cexp"], cexp0 + cid["st0"] + cid["nch"])
                    t.wait_ge(sems["cpool"], cpool0 + hi * 16 + ci + 1)
                    col = 128 * ci
                    while col < S:
                        end = min(S, (col // 512 + 1) * 512)
                        off = col - 128 * ci
                        mi = t.matmul(ps_o[0:65, col:end],
                                      s_vtok[:, ci * VW + 65 * h:ci * VW + 65 * h + 65],
                                      s_pt[:, (ci % 2) * S + off:(ci % 2) * S + off + (end - col)],
                                      start=(ci == 0), stop=(ci == 15),
                                      skip_group_check=True)
                        col = end
                    mi.then_inc(sems["cav"], 1)

                emit_ST(plan["cis"][0])
                for ci in range(1, 16):
                    emit_ST(plan["cis"][ci])
                    emit_AV(plan["cis"][ci - 1])
                emit_AV(plan["cis"][15])
                t.wait_ge(sems["crow"], crow1 + hi + 1)
                for j in range(4):
                    if j >= 2:
                        t.wait_ge(sems["cobc"], cobc0 + hi * 4 + j - 1)
                    t.matmul(ps_ab[j % 2][0:64, 0:512], s_ones[64:65, 0:64],
                             s_rowb[64:65, j * 512:(j + 1) * 512],
                             start=True, stop=True).then_inc(sems["crbc"], 1)

        @block.scalar
        def _(s, b=b, headplan=headplan, cst0=cst0, cav0=cav0,
              s_pt=s_pt, s_scr=s_scr, s_rowb=s_rowb, s_negCh=s_negCh):
            for hi, plan in enumerate(headplan):
                for cid in plan["cis"]:
                    ci = cid["ci"]
                    gci = hi * 16 + ci
                    if gci >= 2:
                        s.wait_ge(sems["cav"], cav0 + gci - 1)
                    for k, (off, L) in enumerate(cid["chs"]):
                        su = cid["st0"] + k
                        s.wait_ge(sems["cst"], cst0 + su + 1)
                        s.activation(s_pt[:, (ci % 2) * S + off:(ci % 2) * S + off + L],
                                     ps_ab[su % 2][:, 0:L], AF.Exp,
                                     bias=s_negCh[:, plan["h"]:plan["h"] + 1]).then_inc(sems["cexp"], 1)
                s.wait_ge(sems["cav"], cav0 + hi * 16 + 16)
                s.activation(s_scr[64:65, 0:S], ps_o[64:65, 0:S], AF.Ln,
                             scale=1.0, bias=s_eps[64:65, :])
                s.activation(s_rowb[64:65, 0:S], s_scr[64:65, 0:S], AF.Exp,
                             scale=-1.0).then_inc(sems["crow"], 1)

        @block.gpsimd
        def _(g, headplan=headplan, cexp0=cexp0, s_pt=s_pt):
            for hi, plan in enumerate(headplan):
                for cid in plan["cis"]:
                    ci = cid["ci"]
                    g.wait_ge(sems["cexp"], cexp0 + cid["st0"] + 1)
                    g.tensor_tensor(s_pt[:, (ci % 2) * S:(ci % 2) * S + 128],
                                    s_pt[:, (ci % 2) * S:(ci % 2) * S + 128],
                                    s_tri[:], ALU.mult).then_inc(sems["cpool"], 1)

        @block.vector
        def _(v, b=b, headplan=headplan, crow1=crow1, crbc1=crbc1, cobc0=cobc0,
              cosh0=cosh0, s_rbc=s_rbc, s_ofm=s_ofm, s_osh=s_osh):
            for hi, plan in enumerate(headplan):
                h = plan["h"]; h2 = h // 2; odd = (h % 2 == 1)
                if odd and hi >= 3:
                    v.wait_ge(sems["cosh"], cosh0 + ((hi - 2) // 2 + 1) * 16)
                for j in range(4):
                    v.wait_ge(sems["crbc"], crbc1 + hi * 4 + j + 1)
                    v.tensor_copy(s_rbc[0:64, (j % 2) * 512:(j % 2 + 1) * 512],
                                  ps_ab[j % 2][0:64, 0:512])
                    dst = (s_osh[0:64, j * 512:(j + 1) * 512] if odd else
                           s_ofm[0:64, h2 * S + j * 512:h2 * S + (j + 1) * 512])
                    v.tensor_tensor(dst, ps_o[0:64, j * 512:(j + 1) * 512],
                                    s_rbc[0:64, (j % 2) * 512:(j % 2 + 1) * 512],
                                    ALU.mult).then_inc(sems["cobc"], 1)

        @block.sync
        def _(sync, headplan=headplan, cobc0=cobc0, s_ofm=s_ofm, s_osh=s_osh):
            for hi, plan in enumerate(headplan):
                if plan["h"] % 2 == 1:
                    h2 = plan["h"] // 2
                    sync.wait_ge(sems["cobc"], cobc0 + hi * 4 + 4)
                    sync.dma_start(out=s_ofm[64:128, h2 * S:(h2 + 1) * S],
                                   in_=s_osh[0:64, 0:S]).then_inc(sems["cosh"], 16)

        c["cst"] += ST_TOTAL
        c["cexp"] += ST_TOTAL
        c["cpool"] += NH * 16
        c["cav"] += NH * 16
        c["crow"] += NH
        c["crbc"] += NH * 4
        c["cobc"] += NH * 4
        c["cosh"] += (NH // 2) * 16

        cwo0, cwoev0 = c["cwo"], c["cwoev"]
        cattn0 = (c["cattn"], c["cattn1"])

        @block.tensor
        def _(t, b=b, cobcA=c["cobc"], coshA=c["cosh"], cwoev0=cwoev0,
              s_ofm=s_ofm, s_wo=s_wo):
            t.wait_ge(sems["cobc"], cobcA)
            t.wait_ge(sems["cosh"], coshA)
            pu = 0
            for tt in range(16):
                for oc5 in range(2):
                    if pu >= 2:
                        t.wait_ge(sems["cwoev"], cwoev0 + pu - 1)
                    pp = ps_ab[pu % 2]
                    for ic in range(8):
                        t.matmul(pp[:, 0:512],
                                 s_ofm[:, ic * S + tt * 128:ic * S + (tt + 1) * 128],
                                 s_wo[:, ic * D + oc5 * 512:ic * D + (oc5 + 1) * 512],
                                 start=(ic == 0), stop=False)
                    mi = t.matmul(pp[:, 0:512], s_ones[0:1, 0:128],
                                  s_bor[0:1, oc5 * 512:(oc5 + 1) * 512],
                                  start=False, stop=True)
                    mi.then_inc(sems["cwo"], 1)
                    pu += 1
        c["cwo"] += 32

        @block.vector
        def _(v, cwo0=cwo0, cattn0=cattn0, s_att=s_att):
            for pu in range(32):
                v.wait_ge(sems["cwo"], cwo0 + pu + 1)
                if pu >= 2:
                    v.wait_ge(sems["cattn" if pu % 2 == 0 else "cattn1"],
                              cattn0[pu % 2] + (pu // 2) * 16)
                v.tensor_copy(s_att[:, (pu % 2) * 512:(pu % 2 + 1) * 512],
                              ps_ab[pu % 2][:, 0:512]).then_inc(sems["cwoev"], 1)
        c["cwoev"] += 32

        @block.sync
        def _(sync, b=b, cwoev0=cwoev0, s_att=s_att):
            pu = 0
            for tt in range(16):
                for oc5 in range(2):
                    sync.wait_ge(sems["cwoev"], cwoev0 + pu + 1)
                    sync.dma_start(
                        out=attn_h[b * S + tt * 128:b * S + (tt + 1) * 128,
                                   oc5 * 512:(oc5 + 1) * 512],
                        in_=s_att[:, (pu % 2) * 512:(pu % 2 + 1) * 512]).then_inc(
                            sems["cattn" if pu % 2 == 0 else "cattn1"], 16)
                    pu += 1
        c["cattn"] += 16 * 16
        c["cattn1"] += 16 * 16

        scC.close()
        scBC.close()

    scD = ExitStack()
    s_wgu = sbt("wgu", [128, 16 * 512], BF16, scD)
    s_wgust = sbt("wgust", [128, 16 * 512], I32, scD)
    s_dxs = sbt("dxs", [128, 2 * 8 * 128], BF16, scD)
    s_hst = sbt("hst", [128, 2 * 512], BF16, scD)
    s_silu2 = sbt("silu2", [128, 2 * 512], F32, scD)

    dwl0, dwc0, dmm0, dsilu0, dhv0 = (c[k] for k in
        ("dwl", "dwc", "dmm", "dsilu", "dhv"))
    dxs0 = (c["dxs"], c["dxs1"])
    dh0 = (c["dh"], c["dh1"])
    NJ = FF // 512
    ps_d = [(ps_a[:, 0:512], ps_b[:, 0:512]), (ps_o[:, 0:512], ps_o[:, 512:1024])]

    @block.gpsimd
    def _(g, dwc0=dwc0, dhv0=dhv0, cattnA=c["cattn"], cattn1A=c["cattn1"], s_hst=s_hst):
        g.wait_ge(sems["cattn"], cattnA)
        g.wait_ge(sems["cattn1"], cattn1A)

        def store(u):
            j, r = u // NT, u % NT
            g.wait_ge(sems["dhv"], dhv0 + u + 1)
            g.dma_start(out=h_h[r * 128:(r + 1) * 128, j * 512:(j + 1) * 512],
                        in_=s_hst[:, (u % 2) * 512:(u % 2 + 1) * 512]).then_inc(
                            sems["dh" if u % 2 == 0 else "dh1"], 16)

        for j in range(NJ):
            if j >= 1:
                g.wait_ge(sems["dwc"], dwc0 + j * 2)
            wgs = W["wgT"][:, j * 512:(j + 1) * 512].rearrange("(ic p) f -> p ic f", p=128)
            wus = W["wuT"][:, j * 512:(j + 1) * 512].rearrange("(ic p) f -> p ic f", p=128)
            g.dma_start(out=s_wgust[:, 0:8 * 512].rearrange("p (ic f) -> p ic f", f=512),
                        in_=wgs).then_inc(sems["dwl"], 16)
            g.dma_start(out=s_wgust[:, 8 * 512:16 * 512].rearrange("p (ic f) -> p ic f", f=512),
                        in_=wus).then_inc(sems["dwl"], 16)
            if j >= 1:
                for r in range(NT):
                    store((j - 1) * NT + r)
        for r in range(NT):
            store((NJ - 1) * NT + r)
    c["dwl"] += NJ * 2 * 16
    c["dh"] += NJ * NT * 8
    c["dh1"] += NJ * NT * 8

    @block.sync
    def _(sync, dmm0=dmm0, cattnA=c["cattn"], cattn1A=c["cattn1"], s_dxs=s_dxs):
        sync.wait_ge(sems["cattn"], cattnA)
        sync.wait_ge(sems["cattn1"], cattn1A)
        for j in range(NJ):
            for r in range(NT):
                u = j * NT + r
                if u >= 2:
                    sync.wait_ge(sems["dmm"], dmm0 + u - 1)
                src = xnT_h[:, r * 128:(r + 1) * 128].rearrange("(ic p) t -> p ic t", p=128)
                sync.dma_start(out=s_dxs[:, (u % 2) * 1024:(u % 2 + 1) * 1024]
                               .rearrange("p (ic t) -> p ic t", t=128),
                               in_=src).then_inc(sems["dxs" if u % 2 == 0 else "dxs1"], 16)
    c["dxs"] += NJ * NT * 8
    c["dxs1"] += NJ * NT * 8

    @block.tensor
    def _(t, dwc0=dwc0, dxs0=dxs0, dsilu0=dsilu0, dhv0=dhv0, s_wgu=s_wgu, s_dxs=s_dxs):
        for j in range(NJ):
            t.wait_ge(sems["dwc"], dwc0 + (j + 1) * 2)
            for r in range(NT):
                u = j * NT + r
                t.wait_ge(sems["dxs" if u % 2 == 0 else "dxs1"],
                          dxs0[u % 2] + (u // 2 + 1) * 16)
                if u >= 2:
                    t.wait_ge(sems["dsilu"], dsilu0 + u - 1)
                    t.wait_ge(sems["dhv"], dhv0 + u - 1)
                pg, pp = ps_d[u % 2]
                for ic in range(8):
                    t.matmul(pg,
                             s_dxs[:, (u % 2) * 1024 + ic * 128:(u % 2) * 1024 + (ic + 1) * 128],
                             s_wgu[:, ic * 512:(ic + 1) * 512],
                             start=(ic == 0), stop=(ic == 7))
                for ic in range(8):
                    mi = t.matmul(pp,
                                  s_dxs[:, (u % 2) * 1024 + ic * 128:(u % 2) * 1024 + (ic + 1) * 128],
                                  s_wgu[:, (8 + ic) * 512:(9 + ic) * 512],
                                  start=(ic == 0), stop=(ic == 7))
                mi.then_inc(sems["dmm"], 1)
    c["dmm"] += NJ * NT

    @block.scalar
    def _(s, dmm0=dmm0, s_silu2=s_silu2):
        for u in range(NJ * NT):
            s.wait_ge(sems["dmm"], dmm0 + u + 1)
            pg, pp = ps_d[u % 2]
            s.activation(s_silu2[:, (u % 2) * 512:(u % 2 + 1) * 512],
                         pg, AF.Silu, scale=1.0 / 127.0).then_inc(sems["dsilu"], 1)
    c["dsilu"] += NJ * NT

    @block.vector
    def _(v, dwl0=dwl0, dmm0=dmm0, dsilu0=dsilu0, dh0=dh0,
          s_wgu=s_wgu, s_wgust=s_wgust, s_hst=s_hst, s_silu2=s_silu2):
        for j in range(NJ):
            v.wait_ge(sems["dwl"], dwl0 + (j + 1) * 32)
            if j >= 1:
                v.wait_ge(sems["dmm"], dmm0 + j * NT)
            v.tensor_copy(s_wgu[:, 0:8 * 512], s_wgust[:, 0:8 * 512])
            v.tensor_copy(s_wgu[:, 8 * 512:16 * 512],
                          s_wgust[:, 8 * 512:16 * 512]).then_inc(sems["dwc"], 2)
            for r in range(NT):
                u = j * NT + r
                v.wait_ge(sems["dsilu"], dsilu0 + u + 1)
                if u >= 2:
                    v.wait_ge(sems["dh" if u % 2 == 0 else "dh1"],
                              dh0[u % 2] + (u // 2) * 16)
                pg, pp = ps_d[u % 2]
                v.scalar_tensor_tensor(s_hst[:, (u % 2) * 512:(u % 2 + 1) * 512],
                                       pp, 1.0 / 127.0,
                                       s_silu2[:, (u % 2) * 512:(u % 2 + 1) * 512],
                                       ALU.mult, ALU.mult).then_inc(sems["dhv"], 1)
    c["dhv"] += NJ * NT
    c["dwc"] += NJ * 2
    scD.close()

    scE = ExitStack()
    s_wd = sbt("wd", [128, 24 * D], BF16, scE)
    s_wdst = sbt("wdst", [128, 2 * D], I32, scE)
    s_ht = sbt("ht", [128, 2 * FF], BF16, scE)
    s_hT = sbt("hT", [128, 2 * FF], BF16, scE)
    s_resid = sbt("resid", [128, 4 * D], F32, scE)
    s_x1st = sbt("x1st", [128, 2 * D], F32, scE)

    ewl0, ewc0, etr0, etev0, emm0, efin0 = (c[k] for k in
        ("ewl", "ewc", "etr", "etev", "emm", "efin"))
    ehl0 = (c["ehl"], c["ehl1"])
    est0 = (c["est"], c["est1"])
    c_etr0 = etr0

    @block.gpsimd
    def _(g, ewc0=ewc0, dhA=c["dh"], dh1A=c["dh1"], dmmA=c["dmm"]):
        g.wait_ge(sems["dh"], dhA)
        g.wait_ge(sems["dh1"], dh1A)
        g.wait_ge(sems["dmm"], dmmA)
        for ic in range(24):
            if ic >= 2:
                g.wait_ge(sems["ewc"], ewc0 + ic - 1)
            g.dma_start(out=s_wdst[:, (ic % 2) * D:(ic % 2 + 1) * D],
                        in_=W["wdT"][ic * 128:(ic + 1) * 128, :]).then_inc(sems["ewl"], 16)
    c["ewl"] += 24 * 16

    @block.vector
    def _(v, ewl0=ewl0, emm0=emm0, efin0=efin0, est0=est0,
          s_wd=s_wd, s_wdst=s_wdst, s_resid=s_resid, s_x1st=s_x1st):
        for ic in range(24):
            v.wait_ge(sems["ewl"], ewl0 + (ic + 1) * 16)
            v.tensor_copy(s_wd[:, ic * D:(ic + 1) * D],
                          s_wdst[:, (ic % 2) * D:(ic % 2 + 1) * D]).then_inc(sems["ewc"], 1)
        for r in range(NT):
            for oc5 in range(2):
                k = r * 2 + oc5
                v.wait_ge(sems["emm"], emm0 + k + 1)
                v.wait_ge(sems["ehl" if r % 2 == 0 else "ehl1"],
                          ehl0[r % 2] + (r // 2 + 1) * 3 * 16)
                if r >= 2 and oc5 == 0:
                    v.wait_ge(sems["est" if r % 2 == 0 else "est1"],
                              est0[r % 2] + (r // 2) * 16)
                v.scalar_tensor_tensor(
                    s_x1st[:, (r % 2) * D + oc5 * 512:(r % 2) * D + (oc5 + 1) * 512],
                    ps_ab[k % 2][:, 0:512], 1.0 / 127.0,
                    s_resid[:, (r % 2) * D + oc5 * 512:(r % 2) * D + (oc5 + 1) * 512],
                    ALU.mult, ALU.add)
                v.tensor_tensor(
                    s_x1st[:, (r % 2) * D + oc5 * 512:(r % 2) * D + (oc5 + 1) * 512],
                    s_x1st[:, (r % 2) * D + oc5 * 512:(r % 2) * D + (oc5 + 1) * 512],
                    s_resid[:, 2 * D + (r % 2) * D + oc5 * 512:2 * D + (r % 2) * D + (oc5 + 1) * 512],
                    ALU.add).then_inc(sems["efin"], 1)
    c["ewc"] += 24
    c["efin"] += NT * 2

    @block.sync
    def _(sync, ehl0=ehl0, efin0=efin0, emm0=emm0, est0=est0, dhA=c["dh"],
          dh1A=c["dh1"], s_ht=s_ht, s_resid=s_resid, s_x1st=s_x1st):
        sync.wait_ge(sems["dh"], dhA)
        sync.wait_ge(sems["dh1"], dh1A)

        def store(r):
            sync.wait_ge(sems["efin"], efin0 + r * 2 + 2)
            sync.dma_start(out=x1[r * 128:(r + 1) * 128, :],
                           in_=s_x1st[:, (r % 2) * D:(r % 2 + 1) * D]).then_inc(
                               sems["est" if r % 2 == 0 else "est1"], 16)

        for r in range(NT):
            if r >= 2:
                sync.wait_ge(sems["etr"], c_etr0 + (r - 1) * 24)
                sync.wait_ge(sems["efin"], efin0 + (r - 1) * 2)
            ehls = sems["ehl" if r % 2 == 0 else "ehl1"]
            sync.dma_start(out=s_ht[:, (r % 2) * FF:(r % 2 + 1) * FF],
                           in_=h_h[r * 128:(r + 1) * 128, :]).then_inc(ehls, 16)
            sync.dma_start(out=s_resid[:, (r % 2) * D:(r % 2 + 1) * D],
                           in_=attn_h[r * 128:(r + 1) * 128, :]).then_inc(ehls, 16)
            sync.dma_start(out=s_resid[:, 2 * D + (r % 2) * D:2 * D + (r % 2 + 1) * D],
                           in_=x[r * 128:(r + 1) * 128, :]).then_inc(ehls, 16)
            if r >= 2:
                store(r - 2)
        store(NT - 2)
        store(NT - 1)
        if final_wait:
            sync.wait_ge(sems["est"], est0[0] + (NT // 2) * 16)
            sync.wait_ge(sems["est1"], est0[1] + (NT // 2) * 16)
    c["ehl"] += (NT // 2) * 3 * 16
    c["ehl1"] += (NT // 2) * 3 * 16
    c["est"] += (NT // 2) * 16
    c["est1"] += (NT // 2) * 16

    @block.scalar
    def _(s, etr0=etr0, s_ht=s_ht, s_hT=s_hT):
        for r in range(NT):
            for fc in range(24):
                k = r * 24 + fc
                s.wait_ge(sems["etr"], etr0 + k + 1)
                s.copy(s_hT[:, (r % 2) * FF + fc * 128:(r % 2) * FF + (fc + 1) * 128],
                       ps_tt[k % 2][:]).then_inc(sems["etev"], 1)
    c["etev"] += NT * 24

    @block.tensor
    def _(t, ewcA=c["ewc"], ehl0=ehl0, etr0=etr0, etev0=etev0, efin0=efin0,
          s_ht=s_ht, s_hT=s_hT, s_wd=s_wd):
        for r in range(NT):
            t.wait_ge(sems["ehl" if r % 2 == 0 else "ehl1"],
                      ehl0[r % 2] + r // 2 * 3 * 16 + 16)
            if r == 0:
                t.wait_ge(sems["ewc"], ewcA)
            for fc in range(24):
                k = r * 24 + fc
                if k >= 2:
                    t.wait_ge(sems["etev"], etev0 + k - 1)
                if fc == 0 and r >= 2:
                    t.wait_ge(sems["emm"], emm0 + (r - 1) * 2)
                t.transpose(ps_tt[k % 2][:],
                            s_ht[:, (r % 2) * FF + fc * 128:(r % 2) * FF + (fc + 1) * 128],
                            s_identb[:]).then_inc(sems["etr"], 1)
            t.wait_ge(sems["etev"], etev0 + r * 24 + 24)
            for oc5 in range(2):
                k2 = r * 2 + oc5
                if k2 >= 2:
                    t.wait_ge(sems["efin"], efin0 + k2 - 1)
                for fc in range(24):
                    mi = t.matmul(ps_ab[k2 % 2][:, 0:512],
                                  s_hT[:, (r % 2) * FF + fc * 128:(r % 2) * FF + (fc + 1) * 128],
                                  s_wd[:, fc * D + oc5 * 512:fc * D + (oc5 + 1) * 512],
                                  start=(fc == 0), stop=(fc == 23))
                mi.then_inc(sems["emm"], 1)
    c["etr"] += NT * 24
    c["emm"] += NT * 2
    scE.close()

    if W.get("_dbg") is not None:
        dbg = W["_dbg"]
        @block.gpsimd
        def _(g, estA=c["est"]):
            g.wait_ge(sems["est"], estA)
            g.dma_start(out=dbg["xnT"][:, :], in_=xnT_h[:, :]).then_inc(sems["ldc"], 16)
            g.dma_start(out=dbg["attn"][:, :], in_=attn_h[:, :]).then_inc(sems["ldc"], 16)
            g.dma_start(out=dbg["h"][:, :], in_=h_h[:, :]).then_inc(sems["ldc"], 16)
            g.wait_ge(sems["ldc"], N_LDC + 48)

    return sems, c


def host_inputs_l0(ref):
    f = np.ascontiguousarray
    W = {}
    W["wqT"] = f(ref["L0_attn_Wq"].T)
    W["wkT"] = f(ref["L0_attn_Wk"].T)
    W["wvT"] = f(ref["L0_attn_Wv"].T)
    W["woT"] = f(ref["L0_attn_Wo"].T)
    W["bqc"] = f(ref["L0_attn_bq"].reshape(8, 128).T)
    W["bkc"] = f(ref["L0_attn_bk"].reshape(8, 128).T)
    W["bv_row"] = f(ref["L0_attn_bv"].reshape(1, D))
    W["bo_row"] = f(ref["L0_attn_bo"].reshape(1, D))
    W["wgT"] = f(ref["L0_ffn_w_gate"].T)
    W["wuT"] = f(ref["L0_ffn_w_up"].T)
    W["wdT"] = f(ref["L0_ffn_w_down"].T)
    W["ident_f32"] = np.eye(128, dtype=np.float32)
    W["ident_bf"] = np.eye(128, dtype=np.float32).astype(ml_dtypes.bfloat16)
    W["ones_bf"] = np.ones((128, 128), dtype=ml_dtypes.bfloat16)
    W["ones_f32"] = np.ones((128, 128), dtype=np.float32)
    tri = np.triu(np.ones((128, 128), dtype=np.float32))
    W["trimask"] = tri.astype(ml_dtypes.bfloat16)
    W["epshuge"] = np.full((128, 1), 1e-35, dtype=np.float32)
    W["eps5"] = np.full((128, 1), 1e-5, dtype=np.float32)
    return W


def build_l0_standalone(dbg=False):
    nc = bass.Bass("TRN2", target_bir_lowering=False, debug=False,
                   detect_race_conditions=False)
    x = nc.dram_tensor("x", [ROWS, D], F32, kind="ExternalInput").ap()
    x1 = nc.dram_tensor("x1", [ROWS, D], F32, kind="ExternalOutput").ap()
    W = {}
    names = [("wqT", [D, D], F32), ("wkT", [D, D], F32), ("wvT", [D, D], F32),
             ("woT", [D, D], F32), ("bqc", [128, 8], F32), ("bkc", [128, 8], F32),
             ("bv_row", [1, D], F32), ("bo_row", [1, D], F32),
             ("wgT", [D, FF], I32), ("wuT", [D, FF], I32), ("wdT", [FF, D], I32),
             ("ident_f32", [128, 128], F32), ("ident_bf", [128, 128], BF16),
             ("ones_bf", [128, 128], BF16), ("ones_f32", [128, 128], F32),
             ("trimask", [128, 128], BF16), ("epshuge", [128, 1], F32),
             ("eps5", [128, 1], F32)]
    for nm, shape, dt in names:
        W[nm] = nc.dram_tensor(nm, shape, dt, kind="ExternalInput").ap()
    if dbg:
        W["_dbg"] = dict(
            xnT=nc.dram_tensor("dbg_xnT", [D, ROWS], BF16, kind="ExternalOutput").ap(),
            attn=nc.dram_tensor("dbg_attn", [ROWS, D], F32, kind="ExternalOutput").ap(),
            h=nc.dram_tensor("dbg_h", [ROWS, FF], BF16, kind="ExternalOutput").ap())
    es = ExitStack()
    with es:
        block = es.enter_context(nc.Block())
        emit_l0(nc, block, es, x, x1, W, pref="l0")
    return nc




F32 = mybir.dt.float32
BF16 = mybir.dt.bfloat16
AF = mybir.ActivationFunctionType
ALU = mybir.AluOpType
AX = mybir.AxisListType

D = 1024
ROWS = 4096
NT = 32
NH = 16
E = 64
TS = 16
HF = 2048
NE = 8
W0W = 196
G = 16


def emit_l1(nc, block, es, x1, x2, W, pref="m1", final_wait=True, dbg=False):
    def sbt(name, shape, dt, scope=None):
        return (scope or es).enter_context(nc.sbuf_tensor(pref + name, shape, dt))

    sems = {}
    for n in ["dbgd", "dbs", "dbd", "ldc", "ldc2", "px", "px1", "pxd", "pz", "pzd", "pzn", "anm", "astd", "rs",
              "wvl", "wvl1", "wvc", "wvs", "wvs1", "bcm", "bcev",
              "swl", "swc", "sxm", "sxe", "sxt", "syt", "sye", "syd",
              "sexq", "sdq", "sbet", "spd", "spdf", "sae", "sab",
              "wol", "woc", "wys", "wys1", "wmm", "wuev", "wa1d", "wa1d1",
              "grl", "grr", "gre",
              "mw1", "mw1c", "mw2", "mw2c", "mb", "mbc",
              "mhid", "mgel", "mtr", "mte", "mdn", "macc",
              "mrl", "mrl1", "mst", "mst1"]:
        sems[n] = es.enter_context(nc.semaphore(pref + n))

    rsC = [0]

    def mkrsw(v):
        def rsw(ins):
            ins.then_inc(sems["rs"], 1)
            rsC[0] += 1
            v.wait_ge(sems["rs"], rsC[0])
        return rsw

    s_identb = sbt("identb", [128, 128], BF16)
    s_ones = sbt("ones", [128, 128], BF16)
    s_onesf = sbt("onesf", [128, 128], F32)
    s_eps5 = sbt("eps5", [128, 1], F32)
    s_g2 = sbt("g2", [128, D], F32)
    s_b2 = sbt("b2", [128, D], F32)
    s_rows = sbt("rows", [1, D], F32)

    ps_a = es.enter_context(nc.psum_tensor(pref + "ps_a", [128, 512], F32))
    ps_b = es.enter_context(nc.psum_tensor(pref + "ps_b", [128, 512], F32))
    ps_h = es.enter_context(nc.psum_tensor(pref + "ps_h", [128, HF], F32))
    ps_t0 = es.enter_context(nc.psum_tensor(pref + "ps_t0", [128, 128], BF16))
    ps_t1 = es.enter_context(nc.psum_tensor(pref + "ps_t1", [128, 128], BF16))
    ps_ab = [ps_a, ps_b]
    ps_tt = [ps_t0, ps_t1]

    dkind = dict(kind="ExternalOutput") if dbg else {}
    if dbg:
        dcct_h = nc.dram_tensor(pref + "dcct", [128, W0W * TS], F32, **dkind).ap()
        dkh_h = nc.dram_tensor(pref + "dkh", [128, TS * E], F32, **dkind).ap()
        dxw0_h = nc.dram_tensor(pref + "dxw0", [128, TS * W0W], F32, **dkind).ap()
        dys_h = nc.dram_tensor(pref + "dys", [128, TS * E], F32, **dkind).ap()
        dwqs_h = nc.dram_tensor(pref + "dwqs", [128, 2 * E], F32, **dkind).ap()
        ddf_h = nc.dram_tensor(pref + "ddf", [128, 512 * TS], F32, **dkind).ap()
    zT_h = nc.dram_tensor(pref + "zT_h", [D, ROWS], BF16, **dkind).ap()
    ysT_h = nc.dram_tensor(pref + "ysT_h", [D, ROWS], BF16, **dkind).ap()
    a1_h = nc.dram_tensor(pref + "a1_h", [ROWS, D], F32, **dkind).ap()
    w1b_h = nc.dram_tensor(pref + "w1b_h", [NE, D, HF], BF16).ap()
    w2b_h = nc.dram_tensor(pref + "w2b_h", [NE, HF, D], BF16).ap()

    scXN = ExitStack()
    s_xn1 = sbt("xn1", [128, NT * D], F32, scXN)

    scW = ExitStack()
    CW = 512
    s_wcv = sbt("wcv", [128, CW], F32, scW)
    s_wcb = sbt("wcb", [128, CW], BF16, scW)
    wchunks = ([("w1", e_, ic, jc) for e_ in range(NE) for ic in range(8)
                for jc in range(HF // CW)] +
               [("w2", e_, ic, jc) for e_ in range(NE) for ic in range(16)
                for jc in range(D // CW)])
    NWC = len(wchunks)

    @block.gpsimd
    def _(g):
        def srcdst(i):
            kind, e_, ic, jc = wchunks[i]
            if kind == "w1":
                return (W["W1"][e_, ic * 128:(ic + 1) * 128, jc * CW:(jc + 1) * CW],
                        w1b_h[e_, ic * 128:(ic + 1) * 128, jc * CW:(jc + 1) * CW])
            return (W["W2"][e_, ic * 128:(ic + 1) * 128, jc * CW:(jc + 1) * CW],
                    w2b_h[e_, ic * 128:(ic + 1) * 128, jc * CW:(jc + 1) * CW])

        for i in range(NWC):
            src, dst = srcdst(i)
            if i >= 1:
                g.wait_ge(sems["wvc"], i)
            g.dma_start(out=s_wcv[:, 0:CW], in_=src).then_inc(sems["wvl"], 16)
            g.wait_ge(sems["wvl"], (i + 1) * 16)
            if i >= 1:
                g.wait_ge(sems["wvs"], i * 16)
            g.tensor_copy(s_wcb[:, 0:CW], s_wcv[:, 0:CW]).then_inc(sems["wvc"], 1)
            g.dma_start(out=dst, in_=s_wcb[:, 0:CW]).then_inc(sems["wvs"], 16)
    N_WVS = NWC * 16
    N_WVS1 = 0

    scP = ExitStack()
    sp_x = sbt("P_x", [128, 2 * D], F32, scP)
    sp_t = sbt("P_t", [128, D], F32, scP)
    sp_t2 = sbt("P_t2", [128, D], F32, scP)
    sp_st = sbt("P_st", [128, 16], F32, scP)
    sp_z = sbt("P_z", [128, D], BF16, scP)
    sp_tst = sbt("P_tst", [128, 8 * 512], BF16, scP)

    @block.sync
    def _(sync):
        for nm, dst in [("ident_bf", s_identb), ("ones_bf", s_ones),
                        ("ones_f32", s_onesf), ("eps5", s_eps5)]:
            sync.dma_start(out=dst[:], in_=W[nm][:, :]).then_inc(sems["ldc"], 16)
        for i, nm in enumerate(["ffn_g", "ffn_b"]):
            if i >= 1:
                sync.wait_ge(sems["bcm"], 2 * i)
            sync.dma_start(out=s_rows[0:1, 0:D], in_=W[nm][:, :]).then_inc(sems["ldc"], 16)

        def zstores(gq):
            sync.wait_ge(sems["pz"], (gq * 4 + 4) * 8)
            for ic in range(8):
                sync.dma_start(out=zT_h[ic * 128:(ic + 1) * 128, gq * 512:(gq + 1) * 512],
                               in_=sp_tst[:, ic * 512:(ic + 1) * 512]).then_inc(sems["pzd"], 16)

        for r in range(NT):
            if r >= 2:
                sync.wait_ge(sems["pxd"], r - 1)
            sync.dma_start(out=sp_x[:, (r % 2) * D:(r % 2 + 1) * D],
                           in_=x1[r * 128:(r + 1) * 128, :]).then_inc(
                               sems["px" if r % 2 == 0 else "px1"], 16)
            if r >= 4 and r % 4 == 0:
                zstores(r // 4 - 1)
        zstores(NT // 4 - 1)
    N_LDC = 6 * 16
    N_PZD = 8 * 8 * 16

    @block.tensor
    def _(t):
        for k in range(4):
            i, j = k // 2, k % 2
            t.wait_ge(sems["ldc"], 4 * 16 + (i + 1) * 16)
            if k >= 2:
                t.wait_ge(sems["bcev"], k - 1)
            t.matmul(ps_ab[k % 2][:, 0:512], s_onesf[0:1, 0:128],
                     s_rows[0:1, j * 512:(j + 1) * 512],
                     start=True, stop=True).then_inc(sems["bcm"], 1)
        for r in range(NT):
            t.wait_ge(sems["pzn"], r + 1)
            if r % 4 == 0 and r > 0:
                t.wait_ge(sems["pzd"], (r // 4) * 8 * 16)
            for ic in range(8):
                k = r * 8 + ic
                if k >= 2:
                    t.wait_ge(sems["pz"], k - 1)
                t.transpose(ps_tt[k % 2][:], sp_z[:, ic * 128:(ic + 1) * 128],
                            s_identb[:]).then_inc(sems["syt"], 1)
    NSP0 = NT * 8

    @block.vector
    def _(v):
        dsts = [s_g2, s_b2]
        for k in range(4):
            v.wait_ge(sems["bcm"], k + 1)
            i, j = k // 2, k % 2
            v.tensor_copy(dsts[i][:, j * 512:(j + 1) * 512],
                          ps_ab[k % 2][:, 0:512]).then_inc(sems["bcev"], 1)
        rsw = mkrsw(v)
        for r in range(NT):
            xt = sp_x[:, (r % 2) * D:(r % 2 + 1) * D]
            v.wait_ge(sems["px" if r % 2 == 0 else "px1"], (r // 2 + 1) * 16)
            rsw(v.tensor_reduce(sp_st[:, 0:1], xt, axis=AX.X, op=ALU.add))
            rsw(v.tensor_scalar(sp_st[:, 1:2], sp_st[:, 0:1], -1.0 / D, None, ALU.mult))
            v.tensor_scalar(sp_t[:], xt, sp_st[:, 1:2], None, ALU.add).then_inc(sems["pxd"], 1)
            v.tensor_tensor(sp_t2[:], sp_t[:], sp_t[:], ALU.mult)
            v.tensor_reduce(sp_st[:, 2:3], sp_t2[:], axis=AX.X,
                            op=ALU.add).then_inc(sems["anm"], 1)
            v.wait_ge(sems["astd"], 2 * r + 1)
            v.tensor_scalar(s_xn1[:, r * D:(r + 1) * D], sp_t[:], sp_st[:, 4:5],
                            None, ALU.mult)
            xnt = s_xn1[:, r * D:(r + 1) * D]
            rsw(v.tensor_reduce(sp_st[:, 5:6], xnt, axis=AX.X, op=ALU.add))
            rsw(v.tensor_scalar(sp_st[:, 6:7], sp_st[:, 5:6], -1.0 / D, None, ALU.mult))
            v.tensor_scalar(sp_t[:], xnt, sp_st[:, 6:7], None, ALU.add)
            v.tensor_tensor(sp_t2[:], sp_t[:], sp_t[:], ALU.mult)
            v.tensor_reduce(sp_st[:, 7:8], sp_t2[:], axis=AX.X,
                            op=ALU.add).then_inc(sems["anm"], 1)
            v.wait_ge(sems["astd"], 2 * r + 2)
            v.tensor_scalar(sp_t2[:], sp_t[:], sp_st[:, 9:10], None, ALU.mult)
            v.tensor_tensor(sp_t[:], sp_t2[:], s_g2[:], ALU.mult)
            if r >= 1:
                v.wait_ge(sems["syt"], r * 8)
            v.tensor_tensor(sp_z[:], sp_t[:], s_b2[:], ALU.add).then_inc(sems["pzn"], 1)

    @block.scalar
    def _(s):
        for r in range(NT):
            s.wait_ge(sems["anm"], 2 * r + 1)
            s.activation(sp_st[:, 3:4], sp_st[:, 2:3], AF.Ln, scale=1.0 / D, bias=s_eps5[:])
            s.activation(sp_st[:, 4:5], sp_st[:, 3:4], AF.Exp,
                         scale=-0.5).then_inc(sems["astd"], 1)
            s.wait_ge(sems["anm"], 2 * r + 2)
            s.activation(sp_st[:, 8:9], sp_st[:, 7:8], AF.Ln, scale=1.0 / D, bias=s_eps5[:])
            s.activation(sp_st[:, 9:10], sp_st[:, 8:9], AF.Exp,
                         scale=-0.5).then_inc(sems["astd"], 1)
            for ic in range(8):
                k = r * 8 + ic
                s.wait_ge(sems["syt"], k + 1)
                s.copy(sp_tst[:, ic * 512 + (r % 4) * 128:ic * 512 + (r % 4) * 128 + 128],
                       ps_tt[k % 2][:]).then_inc(sems["pz"], 1)
    scP.close()

    scS = ExitStack()
    s_w0st = sbt("w0st", [128, 512], F32, scS)
    s_w0h = sbt("w0h", [128, W0W], BF16, scS)
    s_w0m = sbt("w0m", [128, 128], BF16, scS)
    s_xtb = sbt("xtb", [128, TS * E], BF16, scS)
    s_xtT = sbt("xtT", [128, TS * 128], BF16, scS)
    s_xw0 = sbt("xw0", [128, TS * W0W], F32, scS)
    s_wqsv = sbt("wqsv", [128, E], F32, scS)
    s_wksv = sbt("wksv", [128, E], F32, scS)
    s_khist = sbt("khist", [128, TS * E], F32, scS)
    s_cct = sbt("cct", [128, W0W * TS], F32, scS)
    s_tmpd = sbt("tmpd", [128, W0W * (TS - 1)], F32, scS)
    s_mq = sbt("mq", [128, E], F32, scS)
    s_mk = sbt("mk", [128, E], F32, scS)
    s_eq = sbt("eqb", [128, E], F32, scS)
    s_ek = sbt("ekb", [128, E], F32, scS)
    s_a = sbt("sa", [128, TS], F32, scS)
    s_bv = sbt("sbv", [128, TS], F32, scS)
    s_ybc = sbt("ybc", [128, 68], F32, scS)
    s_cl = sbt("cl", [128, 8], F32, scS)
    s_q = sbt("sq_", [128, E], F32, scS)
    s_qk = sbt("sqk", [128, E], F32, scS)
    s_qkb = sbt("sqkb", [128, 2 * E], BF16, scS)
    s_qkT = sbt("sqkT", [128, 2 * 128], BF16, scS)
    s_tb = sbt("stb", [128, 4], F32, scS)
    s_beta = sbt("sbeta", [128, 4], F32, scS)
    s_df = sbt("sdf", [128, W0W], F32, scS)
    s_ys = sbt("sys", [128, TS * E], F32, scS)
    s_ysb = sbt("sysb", [128, TS * E], BF16, scS)
    s_yst = sbt("syst", [128, TS * 128], BF16, scS)

    cct3 = s_cct[:].rearrange("p (j s) -> p j s", s=TS)
    kh3 = s_khist[:].rearrange("p (s e) -> p s e", e=E)

    def xt_ap(ti, st):
        cc, h = ti % 2, ti // 2
        base = (2 * st + cc) * D + E * h
        return s_xn1[:, base:base + E]

    @block.sync
    def _(sync):
        def ys_store(pt):
            cc, hh = pt % 2, pt // 2
            for st in range(TS):
                sync.wait_ge(sems["sye"], 48 * pt + 32 + st + 1)
                sync.dma_start(
                    out=ysT_h[64 * hh:64 * hh + 64,
                              st * 256 + cc * 128:st * 256 + cc * 128 + 128],
                    in_=s_yst[0:64, st * 128:(st + 1) * 128]).then_inc(sems["syd"], 16)

        for ti in range(NT):
            h = ti // 2
            if dbg and ti == 1:
                for st in range(TS):
                    sync.wait_ge(sems["dbs"], st + 1)
                    sync.dma_start(out=ddf_h[:, st * 512:st * 512 + 196],
                                   in_=s_df[:]).then_inc(sems["dbd"], 16)
                    sync.dma_start(out=ddf_h[:, st * 512 + 196:st * 512 + 200],
                                   in_=s_beta[:]).then_inc(sems["dbd"], 16)
                    sync.dma_start(out=ddf_h[:, st * 512 + 200:st * 512 + 216],
                                   in_=s_bv[:]).then_inc(sems["dbd"], 16)
                    sync.dma_start(out=ddf_h[:, st * 512 + 216:st * 512 + 220],
                                   in_=s_tb[:]).then_inc(sems["dbd"], 16)
                    sync.dma_start(out=ddf_h[:, st * 512 + 256:st * 512 + 320],
                                   in_=s_q[:]).then_inc(sems["dbd"], 16)
                    sync.dma_start(out=ddf_h[:, st * 512 + 320:st * 512 + 384],
                                   in_=s_qk[:]).then_inc(sems["dbd"], 16)
                    sync.dma_start(out=ddf_h[:, st * 512 + 384:st * 512 + 448],
                                   in_=s_eq[:]).then_inc(sems["dbd"], 16)
                    sync.dma_start(out=ddf_h[:, st * 512 + 448:st * 512 + 504],
                                   in_=s_ek[:, 0:56]).then_inc(sems["dbd"], 16)
                    sync.dma_start(out=ddf_h[:, st * 512 + 504:st * 512 + 512],
                                   in_=s_cl[:]).then_inc(sems["dbd"], 16)
                sync.wait_ge(sems["sxe"], 19)
                sync.dma_start(out=dcct_h[:, :], in_=s_cct[:]).then_inc(sems["dbgd"], 16)
                sync.dma_start(out=dkh_h[:, :], in_=s_khist[:]).then_inc(sems["dbgd"], 16)
                sync.dma_start(out=dxw0_h[:, :], in_=s_xw0[:]).then_inc(sems["dbgd"], 16)
                sync.dma_start(out=dys_h[:, :], in_=s_ys[:]).then_inc(sems["dbgd"], 16)
                sync.dma_start(out=dwqs_h[:, 0:E], in_=s_wqsv[:]).then_inc(sems["dbgd"], 16)
                sync.dma_start(out=dwqs_h[:, E:2 * E], in_=s_wksv[:]).then_inc(sems["dbgd"], 16)
            if ti >= 1:
                sync.wait_ge(sems["swc"], ti)
            sync.dma_start(out=s_w0st[0:64, 0:W0W],
                           in_=W["W0cat"][h, :, :]).then_inc(sems["swl"], 16)
            sync.dma_start(out=s_w0st[0:64, 256:384],
                           in_=W["W0mT"][h, :, :]).then_inc(sems["swl"], 16)
            if ti >= 1:
                ys_store(ti - 1)
        ys_store(NT - 1)
    N_SYD = NT * TS * 16

    @block.tensor
    def _(t):
        t.wait_ge(sems["pz"], NSP0)
        for ti in range(NT):
            gb = NSP0 + 48 * ti
            for st in range(TS):
                t.wait_ge(sems["sxt"], 16 * ti + st + 1)
                L = 48 * ti + st
                if L >= 2:
                    t.wait_ge(sems["sye"], L - 1)
                t.transpose(ps_tt[(gb + st) % 2][0:64, :], s_xtb[:, st * E:(st + 1) * E],
                            s_identb[:]).then_inc(sems["syt"], 1)
            t.wait_ge(sems["swc"], ti + 1)
            for k in range(18):
                if 18 * ti + k >= 2:
                    t.wait_ge(sems["sxe"], 19 * ti + k - 1 if k >= 2 else 19 * ti)
                if k < 2 and ti >= 1:
                    t.wait_ge(sems["spdf"], 16 * ti)
                if k == 0:
                    t.matmul(ps_ab[0][:, 0:64], s_ones[0:64, 0:128],
                             s_w0m[0:64, 0:64], start=True, stop=True).then_inc(sems["sxm"], 1)
                elif k == 1:
                    t.matmul(ps_ab[1][:, 0:64], s_ones[0:64, 0:128],
                             s_w0m[0:64, 64:128], start=True, stop=True).then_inc(sems["sxm"], 1)
                else:
                    st = k - 2
                    t.wait_ge(sems["sye"], 48 * ti + st + 1)
                    t.matmul(ps_ab[k % 2][:, 0:W0W], s_xtT[0:64, st * 128:(st + 1) * 128],
                             s_w0h[0:64, 0:W0W], start=True, stop=True).then_inc(sems["sxm"], 1)
            for st in range(TS):
                t.wait_ge(sems["sdq"], 16 * ti + st + 1)
                L = 48 * ti + 16 + st
                t.wait_ge(sems["sye"], L - 1)
                t.transpose(ps_tt[(gb + 16 + st) % 2][0:64, :],
                            s_qkb[:, (st % 2) * E:(st % 2 + 1) * E],
                            s_identb[:]).then_inc(sems["syt"], 1)
                t.wait_ge(sems["sye"], L + 1)
                if st == 0:
                    t.wait_ge(sems["sxe"], 19 * ti + 18)
                t.wait_ge(sems["spdf"], 16 * ti + st)
                t.matmul(ps_a[:, 0:W0W], s_qkT[0:64, (st % 2) * 128:(st % 2 + 1) * 128],
                         s_w0h[0:64, 0:W0W], start=True, stop=True).then_inc(sems["spd"], 1)
            t.wait_ge(sems["sxe"], 19 * (ti + 1))
            for st in range(TS):
                L = 48 * ti + 32 + st
                t.wait_ge(sems["sye"], L - 1)
                t.transpose(ps_tt[(gb + 32 + st) % 2][0:64, :], s_ysb[:, st * E:(st + 1) * E],
                            s_identb[:]).then_inc(sems["syt"], 1)

    @block.scalar
    def _(s):
        for ti in range(NT):
            gb = NSP0 + 48 * ti
            for st in range(TS):
                s.wait_ge(sems["syt"], gb + st + 1)
                s.copy(s_xtT[0:64, st * 128:(st + 1) * 128],
                       ps_tt[(gb + st) % 2][0:64, :]).then_inc(sems["sye"], 1)
            for st in range(TS):
                s.wait_ge(sems["sexq"], 16 * ti + st + 1)
                s.activation(s_eq[:], s_mq[:], AF.Exp, bias=s_cl[:, 0:1],
                             accum_out=s_cl[:, 2:3]).then_inc(sems["sae"], 1)
                s.activation(s_ek[:], s_mk[:], AF.Exp, bias=s_cl[:, 1:2],
                             accum_out=s_cl[:, 3:4]).then_inc(sems["sae"], 1)
                s.wait_ge(sems["syt"], gb + 16 + st + 1)
                if st >= 2:
                    s.wait_ge(sems["spd"], 16 * ti + st - 1)
                s.copy(s_qkT[0:64, (st % 2) * 128:(st % 2 + 1) * 128],
                       ps_tt[(gb + 16 + st) % 2][0:64, :]).then_inc(sems["sye"], 1)
                if st == 0:
                    s.wait_ge(sems["sxe"], 19 * ti + 18)
                    s.activation(s_beta[:], s_xw0[:, 64:68],
                                 AF.Sigmoid).then_inc(sems["sab"], 1)
                else:
                    s.wait_ge(sems["sbet"], 15 * ti + st)
                    s.activation(s_beta[:], s_tb[:], AF.Sigmoid).then_inc(sems["sab"], 1)
            for st in range(TS):
                s.wait_ge(sems["syt"], gb + 32 + st + 1)
                if st == 0 and ti >= 1:
                    s.wait_ge(sems["syd"], 256 * ti)
                s.copy(s_yst[0:64, st * 128:(st + 1) * 128],
                       ps_tt[(gb + 32 + st) % 2][0:64, :]).then_inc(sems["sye"], 1)

    @block.vector
    def _(v):
        rsw = mkrsw(v)
        for ti in range(NT):
            if dbg and ti == 1:
                v.wait_ge(sems["dbgd"], 6 * 16)
            for st in range(TS):
                v.tensor_copy(s_xtb[:, st * E:(st + 1) * E],
                              xt_ap(ti, st)).then_inc(sems["sxt"], 1)
            v.wait_ge(sems["swl"], 32 * (ti + 1))
            if ti >= 1:
                v.wait_ge(sems["spd"], 16 * ti)
            rsw(v.tensor_copy(s_w0h[0:64, :], s_w0st[0:64, 0:W0W]))
            v.tensor_copy(s_w0m[0:64, :], s_w0st[0:64, 256:384]).then_inc(sems["swc"], 1)
            for k in range(18):
                v.wait_ge(sems["sxm"], 18 * ti + k + 1)
                if k == 0:
                    v.tensor_copy(s_wqsv[:], ps_ab[0][:, 0:64]).then_inc(sems["sxe"], 1)
                elif k == 1:
                    v.tensor_copy(s_wksv[:], ps_ab[1][:, 0:64]).then_inc(sems["sxe"], 1)
                else:
                    v.tensor_copy(s_xw0[:, (k - 2) * W0W:(k - 1) * W0W],
                                  ps_ab[k % 2][:, 0:W0W]).then_inc(sems["sxe"], 1)
            for st in range(TS):
                xt = xt_ap(ti, st)
                rsw(v.tensor_tensor(s_mq[:], xt, s_wqsv[:], ALU.mult))
                rsw(v.tensor_tensor(s_mk[:], xt, s_wksv[:], ALU.mult))
                if st > 0:
                    xtb3 = xt.unsqueeze(1).broadcast_to([128, st, E])
                    rsw(v.tensor_tensor(s_tmpd[:, 0:st * E].rearrange("p (s e) -> p s e", e=E),
                                        kh3[:, 0:st, :], xtb3, ALU.mult))
                    rsw(v.tensor_reduce(s_a[:, 0:st],
                                        s_tmpd[:, 0:st * E].rearrange("p (s e) -> p s e", e=E),
                                        axis=AX.X, op=ALU.add))
                v.tensor_reduce(s_cl[:, 0:1], s_mq[:], axis=AX.X, op=ALU.max, negate=True)
                v.tensor_reduce(s_cl[:, 1:2], s_mk[:], axis=AX.X, op=ALU.max,
                                negate=True).then_inc(sems["sexq"], 1)
                if st > 0:
                    ab3 = s_a[:, 0:st].unsqueeze(1).broadcast_to([128, 68, st])
                    rsw(v.tensor_tensor(s_tmpd[:, 0:68 * st].rearrange("p (j s) -> p j s", s=st),
                                        cct3[:, 0:68, 0:st], ab3, ALU.mult))
                    rsw(v.tensor_reduce(s_ybc[:],
                                        s_tmpd[:, 0:68 * st].rearrange("p (j s) -> p j s", s=st),
                                        axis=AX.X, op=ALU.add))
                v.wait_ge(sems["sae"], 2 * (16 * ti + st + 1))
                v.reciprocal(s_cl[:, 4:5], s_cl[:, 2:3])
                v.reciprocal(s_cl[:, 5:6], s_cl[:, 3:4])
                if st > 0:
                    v.tensor_tensor(s_ys[:, st * E:(st + 1) * E], s_ybc[:, 0:64],
                                    s_xw0[:, st * W0W:st * W0W + 64], ALU.add)
                else:
                    v.tensor_copy(s_ys[:, 0:E], s_xw0[:, 0:64])
                rsw(v.tensor_scalar(s_q[:], s_eq[:], s_cl[:, 4:5], None, ALU.mult))
                rsw(v.tensor_scalar(s_khist[:, st * E:(st + 1) * E], s_ek[:], s_cl[:, 5:6],
                                    None, ALU.mult))
                rsw(v.tensor_tensor(s_qk[:], s_q[:], s_khist[:, st * E:(st + 1) * E],
                                    ALU.subtract))
                if st >= 2:
                    v.wait_ge(sems["syt"], NSP0 + 48 * ti + 16 + st - 1)
                v.tensor_copy(s_qkb[:, (st % 2) * E:(st % 2 + 1) * E],
                              s_qk[:]).then_inc(sems["sdq"], 1)
                if st > 0:
                    qkb3 = s_qk[:].unsqueeze(1).broadcast_to([128, st, E])
                    rsw(v.tensor_tensor(s_tmpd[:, 0:st * E].rearrange("p (s e) -> p s e", e=E),
                                        kh3[:, 0:st, :], qkb3, ALU.mult))
                    rsw(v.tensor_reduce(s_bv[:, 0:st],
                                        s_tmpd[:, 0:st * E].rearrange("p (s e) -> p s e", e=E),
                                        axis=AX.X, op=ALU.add))
                    v.tensor_tensor(s_tb[:], s_ybc[:, 64:68],
                                    s_xw0[:, st * W0W + 64:st * W0W + 68],
                                    ALU.add).then_inc(sems["sbet"], 1)
                    bb3 = s_bv[:, 0:st].unsqueeze(1).broadcast_to([128, W0W, st])
                    rsw(v.tensor_tensor(s_tmpd[:, 0:W0W * st].rearrange("p (j s) -> p j s", s=st),
                                        cct3[:, :, 0:st], bb3, ALU.mult))
                    rsw(v.tensor_reduce(s_df[:],
                                        s_tmpd[:, 0:W0W * st].rearrange("p (j s) -> p j s", s=st),
                                        axis=AX.X, op=ALU.add))
                    v.wait_ge(sems["spd"], 16 * ti + st + 1)
                    rsw(v.tensor_tensor(s_df[:], ps_a[:, 0:W0W], s_df[:], ALU.add))
                    v.tensor_copy(s_cl[:, 6:7], s_cl[:, 7:8]).then_inc(sems["spdf"], 1)
                else:
                    v.wait_ge(sems["spd"], 16 * ti + 1)
                    rsw(v.tensor_copy(s_df[:], ps_a[:, 0:W0W]))
                    v.tensor_copy(s_cl[:, 6:7], s_cl[:, 7:8]).then_inc(sems["spdf"], 1)
                v.wait_ge(sems["sab"], 16 * ti + st + 1)
                v.tensor_scalar(cct3[:, 68:132, st:st + 1], s_df[:, 68:132],
                                s_beta[:, 1:2], None, ALU.mult)
                v.tensor_scalar(cct3[:, 132:196, st:st + 1], s_df[:, 132:196],
                                s_beta[:, 2:3], None, ALU.mult)
                v.tensor_scalar(cct3[:, 0:64, st:st + 1], s_df[:, 0:64],
                                s_beta[:, 0:1], None, ALU.mult)
                v.tensor_scalar(cct3[:, 64:68, st:st + 1], s_df[:, 64:68],
                                s_beta[:, 3:4], None, ALU.mult)
                rsw(v.tensor_reduce(s_cl[:, 6:7], cct3[:, 68:132, st:st + 1], axis=AX.XY,
                                    op=ALU.add))
                rsw(v.tensor_reduce(s_cl[:, 7:8], cct3[:, 132:196, st:st + 1], axis=AX.XY,
                                    op=ALU.add))
                v.scalar_tensor_tensor(s_wqsv[:], s_khist[:, st * E:(st + 1) * E],
                                       s_cl[:, 6:7], s_wqsv[:], ALU.mult, ALU.add)
                ins = v.scalar_tensor_tensor(s_wksv[:], s_khist[:, st * E:(st + 1) * E],
                                             s_cl[:, 7:8], s_wksv[:], ALU.mult, ALU.add)
                if dbg and ti == 0:
                    ins.then_inc(sems["dbs"], 1)
                    v.wait_ge(sems["dbd"], (st + 1) * 144)
            v.tensor_copy(s_ysb[:], s_ys[:]).then_inc(sems["sxe"], 1)
    N_SDQ = NT * TS
    N_SPD = NT * TS
    N_SYT = NSP0 + NT * 48
    N_SYE = NT * 48
    scS.close()

    scA = ExitStack()
    s_gam = sbt("gam", [128, D], F32, scA)
    s_bet = sbt("bet", [128, D], F32, scA)
    sa_wo = sbt("A_wo", [128, 8 * D], BF16, scA)
    sa_wst = sbt("A_wst", [128, D], F32, scA)
    sa_ysl = sbt("A_ysl", [128, 2 * 8 * 128], BF16, scA)
    sa_u = sbt("A_u", [128, D], F32, scA)
    sa_t = sbt("A_t", [128, D], F32, scA)
    sa_t2 = sbt("A_t2", [128, D], F32, scA)
    sa_st = sbt("A_st", [128, 8], F32, scA)
    sa_a1 = sbt("A_a1", [128, 2 * D], F32, scA)

    @block.gpsimd
    def _(g):
        g.wait_ge(sems["wvs"], N_WVS)
        for i, nm in enumerate(["ln_gamma", "ln_beta"]):
            g.wait_ge(sems["bcm"], 4 + 2 * i)
            g.dma_start(out=s_rows[0:1, 0:D], in_=W[nm][:, :]).then_inc(sems["ldc2"], 16)
        for ic in range(8):
            if ic >= 1:
                g.wait_ge(sems["woc"], ic)
            g.dma_start(out=sa_wst[:],
                        in_=W["w_outT"][ic * 128:(ic + 1) * 128, :]).then_inc(sems["wol"], 16)
            g.wait_ge(sems["wol"], (ic + 1) * 16)
            g.tensor_copy(sa_wo[:, ic * D:(ic + 1) * D], sa_wst[:]).then_inc(sems["woc"], 1)
        g.wait_ge(sems["syd"], N_SYD)
        for r in range(NT):
            if r >= 2:
                g.wait_ge(sems["wmm"], (r - 1) * 2)
            src = ysT_h[:, r * 128:(r + 1) * 128].rearrange("(ic p) t -> p ic t", p=128)
            g.dma_start(out=sa_ysl[:, (r % 2) * 1024:(r % 2 + 1) * 1024]
                        .rearrange("p (ic t) -> p ic t", t=128),
                        in_=src).then_inc(sems["wys" if r % 2 == 0 else "wys1"], 16)

    @block.tensor
    def _(t):
        t.wait_ge(sems["spdf"], NT * TS)
        for k in range(4, 8):
            i, j = (k - 4) // 2, k % 2
            t.wait_ge(sems["ldc2"], (i + 1) * 16)
            t.wait_ge(sems["bcev"], k - 1)
            t.matmul(ps_ab[k % 2][:, 0:512], s_onesf[0:1, 0:128],
                     s_rows[0:1, j * 512:(j + 1) * 512],
                     start=True, stop=True).then_inc(sems["bcm"], 1)
        for r in range(NT):
            t.wait_ge(sems["wys" if r % 2 == 0 else "wys1"], (r // 2 + 1) * 16)
            if r == 0:
                t.wait_ge(sems["woc"], 8)
                t.wait_ge(sems["bcev"], 8)
            for oc5 in range(2):
                k = r * 2 + oc5
                if k >= 2:
                    t.wait_ge(sems["wuev"], k - 1)
                for ic in range(8):
                    mi = t.matmul(ps_ab[k % 2][:, 0:512],
                                  sa_ysl[:, (r % 2) * 1024 + ic * 128:(r % 2) * 1024 + (ic + 1) * 128],
                                  sa_wo[:, ic * D + oc5 * 512:ic * D + (oc5 + 1) * 512],
                                  start=(ic == 0), stop=(ic == 7))
                mi.then_inc(sems["wmm"], 1)

    @block.vector
    def _(v):
        rsw = mkrsw(v)
        dsts2 = [s_gam, s_bet]
        for k in range(4, 8):
            v.wait_ge(sems["bcm"], k + 1)
            i, j = (k - 4) // 2, k % 2
            v.tensor_copy(dsts2[i][:, j * 512:(j + 1) * 512],
                          ps_ab[k % 2][:, 0:512]).then_inc(sems["bcev"], 1)
        for r in range(NT):
            for oc5 in range(2):
                k = r * 2 + oc5
                v.wait_ge(sems["wmm"], k + 1)
                v.tensor_tensor(sa_u[:, oc5 * 512:(oc5 + 1) * 512],
                                ps_ab[k % 2][:, 0:512],
                                s_xn1[:, r * D + oc5 * 512:r * D + (oc5 + 1) * 512],
                                ALU.add).then_inc(sems["wuev"], 1)
            rsw(v.tensor_reduce(sa_st[:, 0:1], sa_u[:], axis=AX.X, op=ALU.add))
            rsw(v.tensor_scalar(sa_st[:, 1:2], sa_st[:, 0:1], -1.0 / D, None, ALU.mult))
            v.tensor_scalar(sa_t[:], sa_u[:], sa_st[:, 1:2], None, ALU.add)
            v.tensor_tensor(sa_t2[:], sa_t[:], sa_t[:], ALU.mult)
            v.tensor_reduce(sa_st[:, 2:3], sa_t2[:], axis=AX.X,
                            op=ALU.add).then_inc(sems["anm"], 1)
            v.wait_ge(sems["astd"], 2 * NT + r + 1)
            v.tensor_scalar(sa_t2[:], sa_t[:], sa_st[:, 4:5], None, ALU.mult)
            v.tensor_tensor(sa_t[:], sa_t2[:], s_gam[:], ALU.mult)
            if r >= 2:
                v.wait_ge(sems["wa1d" if r % 2 == 0 else "wa1d1"], (r // 2) * 16)
            v.tensor_tensor(sa_a1[:, (r % 2) * D:(r % 2 + 1) * D], sa_t[:],
                            s_bet[:], ALU.add)
            v.tensor_tensor(sa_a1[:, (r % 2) * D:(r % 2 + 1) * D],
                            sa_a1[:, (r % 2) * D:(r % 2 + 1) * D],
                            s_xn1[:, r * D:(r + 1) * D], ALU.add).then_inc(sems["pzn"], 1)

    @block.scalar
    def _(s):
        for r in range(NT):
            s.wait_ge(sems["anm"], 2 * NT + r + 1)
            s.activation(sa_st[:, 3:4], sa_st[:, 2:3], AF.Ln, scale=1.0 / D, bias=s_eps5[:])
            s.activation(sa_st[:, 4:5], sa_st[:, 3:4], AF.Exp,
                         scale=-0.5).then_inc(sems["astd"], 1)

    @block.sync
    def _(sync):
        for r in range(NT):
            sync.wait_ge(sems["pzn"], NT + r + 1)
            sync.dma_start(out=a1_h[r * 128:(r + 1) * 128, :],
                           in_=sa_a1[:, (r % 2) * D:(r % 2 + 1) * D]).then_inc(
                               sems["wa1d" if r % 2 == 0 else "wa1d1"], 16)
    N_WA1 = (NT // 2) * 16
    scA.close()
    scW.close()
    scXN.close()

    scM = ExitStack()
    sm_zsl = sbt("M_zsl", [128, G * 8 * 128], BF16, scM)
    sm_acc = sbt("M_acc", [128, G * D], F32, scM)
    sm_w1 = sbt("M_w1", [128, 8 * HF], BF16, scM)
    sm_w2 = sbt("M_w2", [128, 16 * D], BF16, scM)
    sm_rw = sbt("M_rw", [128, 64], BF16, scM)
    sm_ws = sbt("M_ws", [128, 64], F32, scM)
    sm_b1b = sbt("M_b1b", [1, HF], BF16, scM)
    sm_b2b = sbt("M_b2b", [1, D], BF16, scM)
    sm_rbr = sbt("M_rbr", [1, 8], F32, scM)
    sm_rbb = sbt("M_rbb", [1, 8], BF16, scM)
    sm_wts = sbt("M_wts", [128, G * 8], F32, scM)
    sm_hp = sbt("M_hp", [128, HF], BF16, scM)
    sm_hpT = sbt("M_hpT", [128, 16 * 128], BF16, scM)
    sm_res = sbt("M_res", [128, 2 * D], F32, scM)
    sm_x2 = sbt("M_x2", [128, 2 * D], F32, scM)

    NGR = NT // G

    @block.gpsimd
    def _(g):
        g.wait_ge(sems["wa1d"], N_WA1)
        g.wait_ge(sems["wa1d1"], N_WA1)
        g.wait_ge(sems["pzd"], N_PZD)
        g.dma_start(out=sm_ws[:, 0:64].rearrange("p (ic e) -> p ic e", e=8),
                    in_=W["rW"][:, :].rearrange("(ic p) e -> p ic e", p=128)).then_inc(
                        sems["grl"], 16)
        g.dma_start(out=sm_rbr[0:1, :], in_=W["rb"][:, :]).then_inc(sems["grl"], 16)
        for gi in range(NGR):
            if gi >= 1:
                g.wait_ge(sems["mhid"], gi * NE * G * 4)
            for u in range(G):
                r = gi * G + u
                src = zT_h[:, r * 128:(r + 1) * 128].rearrange("(ic p) t -> p ic t", p=128)
                g.dma_start(out=sm_zsl[:, u * 1024:(u + 1) * 1024]
                            .rearrange("p (ic t) -> p ic t", t=128),
                            in_=src).then_inc(sems["grl"], 16)
            for e_ in range(NE):
                ei = gi * NE + e_
                if ei >= 1:
                    g.wait_ge(sems["mhid"], ei * G * 4)
                g.dma_start(out=sm_w1[:].rearrange("p (ic f) -> p ic f", f=HF),
                            in_=w1b_h[e_, :, :].rearrange("(ic p) f -> p ic f", p=128)
                            ).then_inc(sems["mw1"], 16)
                if ei >= 1:
                    g.wait_ge(sems["mdn"], ei * G * 2)
                g.dma_start(out=sm_w2[:].rearrange("p (ic f) -> p ic f", f=D),
                            in_=w2b_h[e_, :, :].rearrange("(ic p) f -> p ic f", p=128)
                            ).then_inc(sems["mw2"], 16)
                if ei >= 1:
                    g.wait_ge(sems["mbc"], 3 * ei)
                g.dma_start(out=s_rows[0:1, 0:D],
                            in_=W["b1"][e_:e_ + 1, 0:D]).then_inc(sems["mb"], 16)
                g.wait_ge(sems["mbc"], 3 * ei + 1)
                g.dma_start(out=s_rows[0:1, 0:D],
                            in_=W["b1"][e_:e_ + 1, D:HF]).then_inc(sems["mb"], 16)
                g.wait_ge(sems["mbc"], 3 * ei + 2)
                g.dma_start(out=s_rows[0:1, 0:D],
                            in_=W["b2"][e_:e_ + 1, :]).then_inc(sems["mb"], 16)

    @block.sync
    def _(sync):
        sync.wait_ge(sems["wa1d"], N_WA1)
        sync.wait_ge(sems["wa1d1"], N_WA1)
        for r in range(NT):
            if r >= 2:
                sync.wait_ge(sems["mw2c"], r - 1)
                sync.dma_start(out=x2[(r - 2) * 128:(r - 1) * 128, :],
                               in_=sm_x2[:, (r % 2) * D:(r % 2 + 1) * D]).then_inc(
                                   sems["mst" if r % 2 == 0 else "mst1"], 16)
                sync.wait_ge(sems["mst" if r % 2 == 0 else "mst1"], (r // 2) * 16)
            sync.dma_start(out=sm_x2[:, (r % 2) * D:(r % 2 + 1) * D],
                           in_=x1[r * 128:(r + 1) * 128, :]).then_inc(
                               sems["mrl" if r % 2 == 0 else "mrl1"], 16)
            sync.dma_start(out=sm_res[:, (r % 2) * D:(r % 2 + 1) * D],
                           in_=a1_h[r * 128:(r + 1) * 128, :]).then_inc(
                               sems["mrl" if r % 2 == 0 else "mrl1"], 16)
        for r in (NT - 2, NT - 1):
            sync.wait_ge(sems["mw2c"], r + 1)
            sync.dma_start(out=x2[r * 128:(r + 1) * 128, :],
                           in_=sm_x2[:, (r % 2) * D:(r % 2 + 1) * D]).then_inc(
                               sems["mst" if r % 2 == 0 else "mst1"], 16)
        if final_wait:
            sync.wait_ge(sems["mst"], (NT // 2) * 16)
            sync.wait_ge(sems["mst1"], (NT // 2) * 16)

    @block.tensor
    def _(t):
        t.wait_ge(sems["wuev"], 2 * NT)
        for gi in range(NGR):
            t.wait_ge(sems["grl"], 32 + (gi + 1) * G * 16)
            if gi == 0:
                t.wait_ge(sems["grr"], 1)
            for u in range(G):
                gu = gi * G + u
                if gu >= 2:
                    t.wait_ge(sems["grr"], gu)
                if u < 2 and gi >= 1:
                    t.wait_ge(sems["macc"], gi * NE * G * 2)
                for ic in range(8):
                    t.matmul(ps_ab[gu % 2][:, 0:8],
                             sm_zsl[:, u * 1024 + ic * 128:u * 1024 + (ic + 1) * 128],
                             sm_rw[:, ic * 8:(ic + 1) * 8], start=(ic == 0), stop=False)
                mi = t.matmul(ps_ab[gu % 2][:, 0:8], s_ones[0:1, 0:128],
                              sm_rbb[0:1, :], start=False, stop=True)
                mi.then_inc(sems["gre"], 1)
            for e_ in range(NE):
                ei = gi * NE + e_
                t.wait_ge(sems["mw1"], (ei + 1) * 16)
                t.wait_ge(sems["mbc"], 3 * (ei + 1))
                for u in range(G):
                    pair = ei * G + u
                    for fj in range(4):
                        if pair * 4 + fj >= 4:
                            t.wait_ge(sems["mgel"], pair * 4 + fj - 3)
                        for ic in range(8):
                            t.matmul(ps_h[:, fj * 512:(fj + 1) * 512],
                                     sm_zsl[:, u * 1024 + ic * 128:u * 1024 + (ic + 1) * 128],
                                     sm_w1[:, ic * HF + fj * 512:ic * HF + (fj + 1) * 512],
                                     start=(ic == 0), stop=False)
                        t.matmul(ps_h[:, fj * 512:(fj + 1) * 512], s_ones[0:1, 0:128],
                                 sm_b1b[0:1, fj * 512:(fj + 1) * 512],
                                 start=False, stop=True).then_inc(sems["mhid"], 1)
                    t.wait_ge(sems["mw2"], (ei + 1) * 16)
                    for fc in range(16):
                        kk = pair * 16 + fc
                        t.wait_ge(sems["mgel"], pair * 4 + fc // 4 + 1)
                        if kk >= 2:
                            t.wait_ge(sems["mte"], kk - 1)
                        t.transpose(ps_tt[kk % 2][:],
                                    sm_hp[:, fc * 128:(fc + 1) * 128],
                                    s_identb[:]).then_inc(sems["mtr"], 1)
                    t.wait_ge(sems["mte"], pair * 16 + 16)
                    if e_ == 0 and u == 0:
                        t.wait_ge(sems["grr"], 1 + (gi + 1) * G)
                    for oc5 in range(2):
                        dk = pair * 2 + oc5
                        if dk >= 2:
                            t.wait_ge(sems["macc"], dk - 1)
                        for fc in range(16):
                            t.matmul(ps_ab[dk % 2][:, 0:512],
                                     sm_hpT[:, fc * 128:(fc + 1) * 128],
                                     sm_w2[:, fc * D + oc5 * 512:fc * D + (oc5 + 1) * 512],
                                     start=(fc == 0), stop=False)
                        t.matmul(ps_ab[dk % 2][:, 0:512], s_ones[0:1, 0:128],
                                 sm_b2b[0:1, oc5 * 512:(oc5 + 1) * 512],
                                 start=False, stop=True).then_inc(sems["mdn"], 1)

    @block.scalar
    def _(s):
        for gi in range(NGR):
            for u in range(G):
                gu = gi * G + u
                s.wait_ge(sems["grr"], gu + 2)
                s.activation(sm_ws[:, 16:24], sm_ws[:, 0:8], AF.Exp,
                             bias=sm_ws[:, 8:9], accum_out=sm_ws[:, 24:25]).then_inc(
                                 sems["mw1c"], 1)
            for e_ in range(NE):
                ei = gi * NE + e_
                for u in range(G):
                    pair = ei * G + u
                    for fj in range(4):
                        s.wait_ge(sems["mhid"], pair * 4 + fj + 1)
                        if pair >= 1:
                            s.wait_ge(sems["mtr"], (pair - 1) * 16 + 16)
                        s.activation(sm_hp[:, fj * 512:(fj + 1) * 512],
                                     ps_h[:, fj * 512:(fj + 1) * 512],
                                     AF.Gelu).then_inc(sems["mgel"], 1)
                    for fc in range(16):
                        kk = pair * 16 + fc
                        s.wait_ge(sems["mtr"], kk + 1)
                        if pair >= 1 and fc == 0:
                            s.wait_ge(sems["mdn"], (pair - 1) * 2 + 2)
                        s.copy(sm_hpT[:, fc * 128:(fc + 1) * 128],
                               ps_tt[kk % 2][:]).then_inc(sems["mte"], 1)

    @block.vector
    def _(v):
        rsw = mkrsw(v)
        v.wait_ge(sems["grl"], 32)
        rsw(v.tensor_copy(sm_rw[:], sm_ws[:, 0:64]))
        v.tensor_copy(sm_rbb[0:1, :], sm_rbr[0:1, :]).then_inc(sems["grr"], 1)
        for gi in range(NGR):
            for u in range(G):
                gu = gi * G + u
                v.wait_ge(sems["gre"], gu + 1)
                v.tensor_copy(sm_ws[:, 0:8], ps_ab[gu % 2][:, 0:8])
                rsw(v.tensor_reduce(sm_ws[:, 8:9], sm_ws[:, 0:8], axis=AX.X,
                                    op=ALU.max, negate=True))
                v.engine_nop() if False else None
                v.tensor_copy(sm_ws[:, 25:26], sm_ws[:, 8:9]).then_inc(sems["grr"], 1)
                v.wait_ge(sems["mw1c"], gu + 1)
                rsw(v.tensor_reduce(sm_ws[:, 26:27], sm_ws[:, 16:24], axis=AX.X, op=ALU.max))
                v.tensor_scalar(sm_ws[:, 32:40], sm_ws[:, 16:24], sm_ws[:, 26:27],
                                None, ALU.is_equal)
                v.tensor_tensor(sm_ws[:, 40:48], sm_ws[:, 16:24], sm_ws[:, 32:40], ALU.mult)
                v.tensor_tensor(sm_ws[:, 48:56], sm_ws[:, 16:24], sm_ws[:, 40:48], ALU.subtract)
                rsw(v.tensor_reduce(sm_ws[:, 27:28], sm_ws[:, 48:56], axis=AX.X, op=ALU.max))
                v.tensor_scalar(sm_ws[:, 56:64], sm_ws[:, 48:56], sm_ws[:, 27:28],
                                None, ALU.is_equal)
                v.tensor_tensor(sm_ws[:, 32:40], sm_ws[:, 32:40], sm_ws[:, 56:64], ALU.add)
                v.tensor_tensor(sm_ws[:, 40:48], sm_ws[:, 16:24], sm_ws[:, 32:40], ALU.mult)
                rsw(v.tensor_tensor(sm_ws[:, 28:29], sm_ws[:, 26:27], sm_ws[:, 27:28],
                                    ALU.add))
                rsw(v.reciprocal(sm_ws[:, 29:30], sm_ws[:, 28:29]))
                v.tensor_scalar(sm_wts[:, u * 8:(u + 1) * 8], sm_ws[:, 40:48],
                                sm_ws[:, 29:30], None, ALU.mult)
            for e_ in range(NE):
                ei = gi * NE + e_
                v.wait_ge(sems["mb"], ei * 48 + 16)
                v.tensor_copy(sm_b1b[0:1, 0:D], s_rows[0:1, 0:D]).then_inc(sems["mbc"], 1)
                v.wait_ge(sems["mb"], ei * 48 + 32)
                v.tensor_copy(sm_b1b[0:1, D:HF], s_rows[0:1, 0:D]).then_inc(sems["mbc"], 1)
                v.wait_ge(sems["mb"], ei * 48 + 48)
                v.tensor_copy(sm_b2b[0:1, 0:D], s_rows[0:1, 0:D]).then_inc(sems["mbc"], 1)
                for u in range(G):
                    pair = ei * G + u
                    for oc5 in range(2):
                        dk = pair * 2 + oc5
                        v.wait_ge(sems["mdn"], dk + 1)
                        dst = sm_acc[:, u * D + oc5 * 512:u * D + (oc5 + 1) * 512]
                        wcol = sm_wts[:, u * 8 + e_:u * 8 + e_ + 1]
                        if e_ == 0:
                            v.tensor_scalar(dst, ps_ab[dk % 2][:, 0:512], wcol,
                                            None, ALU.mult).then_inc(sems["macc"], 1)
                        else:
                            v.scalar_tensor_tensor(dst, ps_ab[dk % 2][:, 0:512], wcol,
                                                   dst, ALU.mult, ALU.add).then_inc(sems["macc"], 1)
            for u in range(G):
                r = gi * G + u
                v.wait_ge(sems["macc"], ((gi * NE + NE - 1) * G + u + 1) * 2)
                v.wait_ge(sems["mrl" if r % 2 == 0 else "mrl1"], (r // 2 + 1) * 32)
                if r >= 2:
                    v.wait_ge(sems["mst" if r % 2 == 0 else "mst1"], (r // 2) * 16)
                v.tensor_tensor(sm_x2[:, (r % 2) * D:(r % 2 + 1) * D],
                                sm_x2[:, (r % 2) * D:(r % 2 + 1) * D],
                                sm_acc[:, u * D:(u + 1) * D], ALU.add)
                v.tensor_tensor(sm_x2[:, (r % 2) * D:(r % 2 + 1) * D],
                                sm_x2[:, (r % 2) * D:(r % 2 + 1) * D],
                                sm_res[:, (r % 2) * D:(r % 2 + 1) * D],
                                ALU.add).then_inc(sems["mw2c"], 1)
    scM.close()
    return sems


def host_inputs_l1(ref):
    f = np.ascontiguousarray
    W = {}
    W["w_outT"] = f(ref["L1_attn_w_out"].T)
    W["ln_gamma"] = f(ref["L1_attn_ln_gamma"].reshape(1, D))
    W["ln_beta"] = f(ref["L1_attn_ln_beta"].reshape(1, D))
    W["ffn_g"] = f(ref["L1_ffn_ln_g"].reshape(1, D))
    W["ffn_b"] = f(ref["L1_ffn_ln_b"].reshape(1, D))
    W["rW"] = f(ref["L1_ffn_rW"])
    W["rb"] = f(ref["L1_ffn_rb"].reshape(1, 8))
    W["W1"] = f(ref["L1_ffn_W1"])
    W["b1"] = f(ref["L1_ffn_b1"])
    W["W2"] = f(ref["L1_ffn_W2"])
    W["b2"] = f(ref["L1_ffn_b2"])
    wy = ref["L1_attn_weight_y"][0]
    wq = ref["L1_attn_weight_q"][0]
    wk = ref["L1_attn_weight_k"][0]
    wb = ref["L1_attn_weight_beta"][0]
    W["W0cat"] = f(np.concatenate([wy, wb, wq, wk], axis=2))
    W["W0mT"] = f(np.concatenate([wq.transpose(0, 2, 1), wk.transpose(0, 2, 1)],
                                 axis=2))
    W["ident_bf"] = np.eye(128, dtype=np.float32).astype(ml_dtypes.bfloat16)
    W["ones_bf"] = np.ones((128, 128), dtype=ml_dtypes.bfloat16)
    W["ones_f32"] = np.ones((128, 128), dtype=np.float32)
    W["eps5"] = np.full((128, 1), 1e-5, dtype=np.float32)
    return W


L1_NAMES = [("w_outT", [D, D], F32), ("ln_gamma", [1, D], F32), ("ln_beta", [1, D], F32),
            ("ffn_g", [1, D], F32), ("ffn_b", [1, D], F32), ("rW", [D, 8], F32),
            ("rb", [1, 8], F32), ("W1", [NE, D, HF], F32), ("b1", [NE, HF], F32),
            ("W2", [NE, HF, D], F32), ("b2", [NE, D], F32),
            ("W0cat", [NH, E, W0W], F32), ("W0mT", [NH, E, 128], F32),
            ("ident_bf", [128, 128], BF16), ("ones_bf", [128, 128], BF16),
            ("ones_f32", [128, 128], F32), ("eps5", [128, 1], F32)]


def build_l1_standalone(dbg=False):
    nc = bass.Bass("TRN2", target_bir_lowering=False, debug=False,
                   detect_race_conditions=False)
    x1 = nc.dram_tensor("x1", [ROWS, D], F32, kind="ExternalInput").ap()
    x2 = nc.dram_tensor("x2", [ROWS, D], F32, kind="ExternalOutput").ap()
    W = {}
    for nm, shape, dt in L1_NAMES:
        W[nm] = nc.dram_tensor(nm, shape, dt, kind="ExternalInput").ap()
    es = ExitStack()
    with es:
        block = es.enter_context(nc.Block())
        emit_l1(nc, block, es, x1, x2, W, pref="m1", dbg=dbg)
    return nc


def kernel(**inputs):
    ref = {k: np.asarray(v) for k, v in inputs.items()}
    nc0 = build_l0_standalone()
    W0 = host_inputs_l0(ref)
    xfull = np.ascontiguousarray(ref["L0_attn_x"].astype(np.float32, copy=False))
    in_maps = []
    for c in range(8):
        m = dict(W0)
        m["x"] = np.ascontiguousarray(xfull[2 * c:2 * c + 2].reshape(4096, 1024))
        in_maps.append(m)
    res0 = run_bass_kernel_spmd(nc0, in_maps, core_ids=list(range(8)))
    x1 = np.concatenate(
        [np.asarray(res0.results[c]["x1"]).reshape(2, 2048, 1024) for c in range(8)],
        axis=0)
    nc1 = build_l1_standalone()
    W1 = host_inputs_l1(ref)
    in_maps = []
    for c in range(8):
        m = dict(W1)
        m["x1"] = np.ascontiguousarray(
            x1[:, 256 * c:256 * (c + 1), :].reshape(4096, 1024))
        in_maps.append(m)
    res1 = run_bass_kernel_spmd(nc1, in_maps, core_ids=list(range(8)))
    out = np.zeros((16, 2048, 1024), dtype=np.float32)
    for c in range(8):
        out[:, 256 * c:256 * (c + 1), :] = np.asarray(
            res1.results[c]["x2"]).reshape(16, 256, 1024)
    return out
